# version 22
# baseline (speedup 1.0000x reference)
import numpy as np
import concourse.bass as bass
import concourse.mybir as mybir
from concourse.bass_utils import run_bass_kernel_spmd

F32 = mybir.dt.float32
BF16 = mybir.dt.bfloat16
I32 = mybir.dt.int32
AX = mybir.AxisListType
AF = mybir.ActivationFunctionType
OP = mybir.AluOpType

NCORES = 8
S = 8192
D = 1024
OWN = 4096
QG = 512
NQG = OWN // QG
KT = 1024
NKT = S // KT
DC = D // 128
NT_G = QG // 128
LN2C = float(np.log(2.0))
EPS = 1e-5
TINY = 1e-30
QSCALE = 1.0 / 32.0
NG1 = S // 512
NOWN1 = OWN // 512


class Sem:
    def __init__(self, nc, name):
        self.h = nc.alloc_semaphore(name)
        self.count = 0

    def inc(self, n=1):
        self.count += n
        return self.count


class Plan:
    def __init__(self):
        self.steps = {k: [] for k in ("sync", "gpsimd", "vector", "scalar",
                                      "tensor")}

    serialize = ()
    serialize_eng = None

    def add(self, eng, fn, waits=(), inc=None, n=1, extra=0):
        if self.serialize and (self.serialize_eng is None
                               or (eng == self.serialize_eng[1:]
                                   if self.serialize_eng[0] == "+"
                                   else eng != self.serialize_eng[1:])):
            waits = list(waits) + [(s, s.count) for s in self.serialize
                                   if s.count > 0]
        tok = inc.inc(n + extra) if inc is not None else None
        self.steps[eng].append((tuple(waits), fn, inc, n))
        return tok

    def emit(self, block):
        def mk(name):
            sep = name == "vector"

            def run(e):
                first = True
                for waits, fn, inc, n in self.steps[name]:
                    if sep and not first:
                        e.drain()
                    first = False
                    for sem, val in waits:
                        if val and val > 0:
                            e.wait_ge(sem.h, val)
                    inst = fn(e)
                    if inc is not None:
                        inst.then_inc(inc.h, n)
            return run
        block.sync(mk("sync"))
        block.gpsimd(mk("gpsimd"))
        block.vector(mk("vector"))
        block.scalar(mk("scalar"))
        block.tensor(mk("tensor"))


def build_program(debug_taps=()):
    nc = bass.Bass("TRN2", target_bir_lowering=False)

    x_in = nc.dram_tensor("x_in", [S, D], F32, kind="ExternalInput").ap()
    w_q = nc.dram_tensor("w_q", [D, D], F32, kind="ExternalInput").ap()
    w_k = nc.dram_tensor("w_k", [D, D], F32, kind="ExternalInput").ap()
    w_v = nc.dram_tensor("w_v", [D, D], F32, kind="ExternalInput").ap()
    b_q = nc.dram_tensor("b_q", [D], F32, kind="ExternalInput").ap()
    b_k = nc.dram_tensor("b_k", [D], F32, kind="ExternalInput").ap()
    b_v = nc.dram_tensor("b_v", [D], F32, kind="ExternalInput").ap()
    w_f = nc.dram_tensor("w_f", [D, D], F32, kind="ExternalInput").ap()
    b_f = nc.dram_tensor("b_f", [D], F32, kind="ExternalInput").ap()
    d_w = nc.dram_tensor("d_w", [D, 32], F32, kind="ExternalInput").ap()
    d_a = nc.dram_tensor("d_a", [1], F32, kind="ExternalInput").ap()
    out = nc.dram_tensor("out", [OWN, D], F32, kind="ExternalOutput").ap()

    q_scr = nc.dram_tensor("q_scr", [DC, 128, OWN], BF16).ap()
    k_scr = nc.dram_tensor("k_scr", [NKT, DC, 128, KT], BF16).ap()
    v_scr = nc.dram_tensor("v_scr", [S // 128, 128, D], BF16).ap()

    taps = {}
    for name in debug_taps:
        shp = {"h1": [OWN, D], "qq": [DC, 128, OWN], "kk": [NKT, DC, 128, KT],
               "vv": [S // 128, 128, D], "wqt": [128, DC * D],
               "wq0": [128, D], "e0": [128, 32], "rsc0": [128, 32],
               "psc0": [128, 32], "am0": [128, 32], "cw0": [128, 32]}[name]
        dt = BF16 if name in ("qq", "kk", "vv", "wqt", "wq0") else F32
        taps[name] = nc.dram_tensor("tap_" + name, shp, dt,
                                    kind="ExternalOutput").ap()

    A = nc.alloc_sbuf_tensor

    AR = A("arena", [128, 46080], BF16)

    def arv(off, n):
        return AR[:, off:off + n]

    w3 = [arv(i * 8192, 8192) for i in range(3)]
    xtm = arv(24576, 4096)
    xfm = [arv(28672 + p * 4096, 4096) for p in range(2)]
    kqstage = arv(36864, 4096)
    vstage = arv(40960, 4096)
    wq_tm = arv(45056, 1024)
    q_fm = arv(0, DC * QG)
    k_sb = [arv(4096 + p * 8192, 8192) for p in range(2)]
    v_sb = [arv(20480 + p * 8192, 8192) for p in range(2)]
    at_sb = [arv(36864 + p * 1024, 1024) for p in range(2)]
    att_sb = [arv(38912 + p * 1024, 1024) for p in range(2)]
    pa_sb = [arv(40960 + p * 1024, 1024) for p in range(2)]
    xqt_sb = arv(43008, 1024)

    ident = A("ident", [128, 128], BF16)
    bq_sb = A("bq_sb", [128, DC], F32)
    bq32_sb = A("bq32_sb", [128, DC], F32)
    bk_sb = A("bk_sb", [128, DC], F32)
    bf_sb = A("bf_sb", [128, D], F32)
    ones_r = A("ones_r", [1, 128], BF16)
    bv_row = A("bv_row", [1, D], BF16)
    da_sb = A("da_sb", [128, 1], F32)
    c_a = A("c_a", [128, 1], F32)
    bias_act = A("bias_act", [128, 1], F32)
    cb_p2l = A("cb_p2l", [128, 1], F32)
    cb_n2l = A("cb_n2l", [128, 1], F32)
    cb_eps = A("cb_eps", [128, 1], F32)

    wf_t = A("wf_t", [128, DC * D], BF16)
    wfr = A("wfr", [128, D], F32)
    dw_sb = A("dw_sb", [128, DC * 32], F32)

    qs_f = [A(f"qs_f{i}", [128, D], F32) for i in range(6)]
    qs_i = A("qs_i", [128, D], I32)
    sq_dump = A("sq_dump", [128, D], F32)
    be_f = [A(f"be_f{i}", [128, 32], F32) for i in range(4)]
    be_i = [A(f"be_i{i}", [128, 32], I32) for i in range(2)]
    wt_f = [A(f"wt_f{i}", [128, 32], F32) for i in range(3)]
    cw_t = A("cw_t", [128, 32], F32)
    rcw_t = A("rcw_t", [128, 32], F32)
    am_t = A("am_t", [128, 32], F32)
    e_t = A("e_t", [128, 32], F32)
    rsc_t = A("rsc_t", [128, 32], F32)
    psc_t = A("psc_t", [128, 32], F32)

    o_acc = [A(f"o_acc{p}", [128, NT_G * D], F32) for p in range(2)]
    l_cur = [A(f"l_cur{p}", [128, NT_G], F32) for p in range(2)]
    negm_g = A("negm_g", [128, NT_G], F32)
    linv_g = A("linv_g", [128, NT_G], F32)
    lt_b = [A(f"lt_b{p}", [128, 1], F32) for p in range(3)]
    lt_g = A("lt_g", [128, 1], F32)

    x_res = A("x_res", [128, D], F32)
    h_sb = [A(f"h_sb{p}", [128, D], F32) for p in range(2)]
    t2_sb = A("t2_sb", [128, D], F32)
    out_sb = A("out_sb", [128, D], F32)
    st_f = [A(f"st_f{i}", [128, 1], F32) for i in range(6)]

    AP_ = nc.alloc_psum_tensor
    s_ps = AP_("s_ps", [128, KT], F32)
    t_ps = AP_("t_ps", [128, KT], BF16)
    o_ps = [AP_(f"o_ps{p}", [128, D], F32) for p in range(2)]
    f_ps = AP_("f_ps", [128, 512], F32)
    p1_ps = [o_ps[0][:, 0:512], o_ps[0][:, 512:1024]]

    dIN = Sem(nc, "dIN")
    dGIN = Sem(nc, "dGIN")
    dX1 = Sem(nc, "dX1")
    dSCR = Sem(nc, "dSCR")
    dKV = Sem(nc, "dKV")
    dKVV = Sem(nc, "dKVV")
    dQL = Sem(nc, "dQL")
    dXR = Sem(nc, "dXR")
    dOUT = Sem(nc, "dOUT")
    dTAP = Sem(nc, "dTAP") if debug_taps else None
    tXT = Sem(nc, "tXT")
    vXE = Sem(nc, "vXE")
    tPJ = Sem(nc, "tPJ")
    vPJ = Sem(nc, "vPJ")
    sPJ = Sem(nc, "sPJ")
    tS = Sem(nc, "tS")
    vR = Sem(nc, "vR")
    sE = Sem(nc, "sE")
    sA = Sem(nc, "sA")
    tT = Sem(nc, "tT")
    sTE = Sem(nc, "sTE")
    tAV = Sem(nc, "tAV")
    vO = Sem(nc, "vO")
    vQ = Sem(nc, "vQ")
    sQ = Sem(nc, "sQ")
    tF = Sem(nc, "tF")
    gID = Sem(nc, "gID")

    pl = Plan()

    t_id0 = pl.add("gpsimd", lambda e: e.memset(ident[:, :], 0.0), inc=gID)
    tok_ident = pl.add("gpsimd", lambda e: e.affine_select(
        out=ident[:, :], in_=ident[:, :], compare_op=OP.not_equal, fill=1.0,
        base=0, pattern=[[-1, 128]], channel_multiplier=1),
        waits=[(gID, t_id0)], inc=gID)

    for src, dst in ((b_q, bq_sb), (b_k, bk_sb)):
        pl.add("sync", lambda e, s=src, d=dst: e.dma_start(
            out=d[:, :], in_=s.rearrange("(c p) -> p c", p=128),
            allow_slow_non_contiguous=True),
            inc=dIN, n=16)
    pl.add("gpsimd", lambda e: e.dma_start(
        out=bv_row[:, :], in_=b_v.unsqueeze(0)), inc=dGIN, n=16)
    pl.add("gpsimd", lambda e: e.memset(ones_r[:, :], 1.0), inc=None)
    pl.add("sync", lambda e: e.dma_start(
        out=bf_sb[:, :],
        in_=b_f.unsqueeze(0).broadcast_to([128, D])), inc=dIN, n=16)
    pl.add("sync", lambda e: e.dma_start(
        out=da_sb[:, :],
        in_=d_a.unsqueeze(0).broadcast_to([128, 1])), inc=dIN, n=16)
    for i, w in enumerate((w_q, w_k, w_v)):
        pl.add("gpsimd", lambda e, w=w, i=i: e.dma_start(
            out=w3[i].rearrange("p (c o) -> p c o", c=DC),
            in_=w.rearrange("(c p) o -> p c o", p=128)), inc=dGIN, n=16)
    tok_dw = pl.add("sync", lambda e: e.dma_start(
        out=dw_sb.ap().rearrange("p (c b) -> p c b", c=DC),
        in_=d_w.rearrange("(c p) b -> p c b", p=128)), inc=dIN, n=16)
    tok_din = dIN.count
    tok_gin = dGIN.count
    tok_bq32 = pl.add("vector", lambda e: e.tensor_single_scalar(
        bq32_sb[:, :], bq_sb[:, :], QSCALE, OP.mult),
        waits=[(dIN, tok_din)], inc=vQ)

    pl.add("vector", lambda e: e.memset(cb_p2l[:, :], 1.0), inc=vQ)
    pl.add("vector", lambda e: e.tensor_single_scalar(
        cb_p2l[:, :], cb_p2l[:, :], 2.0 * LN2C, OP.mult), inc=vQ)
    pl.add("vector", lambda e: e.memset(cb_n2l[:, :], 1.0), inc=vQ)
    pl.add("vector", lambda e: e.tensor_single_scalar(
        cb_n2l[:, :], cb_n2l[:, :], -2.0 * LN2C, OP.mult), inc=vQ)
    pl.add("vector", lambda e: e.memset(cb_eps[:, :], EPS), inc=vQ)
    tok_bq32 = None

    tok = pl.add("scalar", lambda e: e.activation(
        st_f[0][:, :], da_sb[:, :], AF.Exp, scale=-8.0),
        waits=[(dIN, tok_din)], inc=sQ)
    pl.add("vector", lambda e: e.tensor_scalar(
        st_f[1][:, :], st_f[0][:, :], -1.0, 1.0, OP.mult, OP.add),
        waits=[(sQ, tok)], inc=vQ)
    pl.add("vector", lambda e: e.tensor_scalar(
        st_f[2][:, :], st_f[0][:, :], 1.0, None, OP.add), inc=vQ)
    def _recip(e, out, in_):
        inst = e.reciprocal(out, in_)
        e.drain()
        return inst

    pl.add("vector", lambda e: _recip(e, st_f[3][:, :], st_f[2][:, :]),
           inc=vQ)
    tok_tanh = pl.add("vector", lambda e: e.tensor_mul(
        st_f[4][:, :], st_f[1][:, :], st_f[3][:, :]), inc=vQ)
    tok_ca = pl.add("scalar", lambda e: e.activation(
        c_a[:, :], st_f[4][:, :], AF.Exp, scale=0.5 * LN2C),
        waits=[(vQ, tok_tanh)], inc=sQ)
    pl.add("vector", lambda e: e.tensor_scalar(
        bias_act[:, :], st_f[4][:, :], -0.5 * LN2C, 2.0 * LN2C,
        OP.mult, OP.add), inc=vQ)

    def emit_block_exponent(src_am, rsc_bias, extra_rsc, extra_psc, wait_s,
                            box):
        b0, b1, b2, b3 = (t[:, :] for t in be_f)
        i0, i1 = (t[:, :] for t in be_i)
        pl.add("vector", lambda e: e.tensor_single_scalar(
            b0, src_am[:, :], TINY, OP.max), inc=vQ)
        yield
        pl.add("vector", lambda e: e.tensor_single_scalar(
            i0, be_f[0].ap().bitcast(I32), 0x7F800000, OP.bitwise_and),
            inc=vQ)
        yield
        pl.add("vector", lambda e: e.tensor_copy(
            b1, be_i[0].ap().bitcast(F32)), inc=vQ)
        yield
        pl.add("vector", lambda e: e.tensor_single_scalar(
            i1, i0, 23, OP.logical_shift_right), inc=vQ)
        yield
        pl.add("vector", lambda e: e.tensor_single_scalar(
            i0, i1, 127, OP.subtract), inc=vQ)
        yield
        pl.add("vector", lambda e: e.tensor_copy(b2, i0), inc=vQ)
        yield
        pl.add("vector", lambda e: e.tensor_single_scalar(
            b3, b1, 2.0, OP.mult), inc=vQ)
        yield
        pl.add("vector", lambda e: e.tensor_tensor(
            b3, b0, b3, OP.is_ge), inc=vQ)
        yield
        pl.add("vector", lambda e: e.tensor_tensor(
            b1, b0, b1, OP.is_lt), inc=vQ)
        yield
        pl.add("vector", lambda e: e.scalar_tensor_tensor(
            b0, b1, -1.0, b3, OP.mult, OP.add), inc=vQ)
        yield
        tok_e = pl.add("vector", lambda e: e.tensor_tensor(
            e_t[:, :], b2, b0, OP.add), inc=vQ)
        yield
        bias = rsc_bias[:, :]
        t1 = pl.add("scalar", lambda e, bias=bias: e.activation(
            rsc_t[:, :], e_t[:, :], AF.Exp, scale=-LN2C, bias=bias),
            waits=[(vQ, tok_e)] + list(wait_s), inc=sQ)
        yield
        t2 = pl.add("scalar", lambda e: e.activation(
            psc_t[:, :], e_t[:, :], AF.Exp, scale=LN2C, bias=cb_n2l[:, :]),
            inc=sQ)
        yield
        box["tok_s"] = t2
        box["tok_v"] = None
        if extra_rsc is not None:
            pl.add("vector", lambda e: e.tensor_mul(
                be_f[0][:, :], rsc_t[:, :], extra_rsc[:, :]),
                waits=[(sQ, t1)], inc=vQ)
            yield
            pl.add("vector", lambda e: e.tensor_copy(
                rsc_t[:, :], be_f[0][:, :]), inc=vQ)
            yield
            pl.add("vector", lambda e: e.tensor_mul(
                be_f[0][:, :], psc_t[:, :], extra_psc[:, :]),
                waits=[(sQ, t2)], inc=vQ)
            yield
            box["tok_v"] = pl.add("vector", lambda e: e.tensor_copy(
                psc_t[:, :], be_f[0][:, :]), inc=vQ)
            yield

    def emit_quant_core(src, dst, wait_first, box):
        S1, S2, S3, S4, S5, S6 = (t[:, :] for t in qs_f)
        I_ = qs_i[:, :]
        v3 = lambda t: t.rearrange("p (b k) -> p b k", k=32)
        bc = lambda t: t.ap().unsqueeze(2).broadcast_to([128, 32, 32])
        pl.add("vector", lambda e: e.tensor_tensor(
            v3(S1), v3(src), bc(rsc_t), OP.mult), waits=wait_first, inc=vQ)
        yield
        tok_clip = pl.add("vector", lambda e: e.tensor_scalar(
            S2, S1, 6.0, -6.0, OP.min, OP.max), inc=vQ)
        yield
        tok_aa = pl.add("scalar", lambda e: e.activation(
            qs_f[0][:, :], qs_f[1][:, :], AF.Abs),
            waits=[(vQ, tok_clip)], inc=sQ)
        yield
        pl.add("vector", lambda e: e.tensor_single_scalar(
            S4, S1, 2.0, OP.is_ge), waits=[(sQ, tok_aa)], inc=vQ)
        yield
        pl.add("vector", lambda e: e.tensor_single_scalar(
            S5, S1, 4.0, OP.is_ge), inc=vQ)
        yield
        pl.add("vector", lambda e: e.scalar_tensor_tensor(
            S6, S4, 0.5, S5, OP.mult, OP.add), inc=vQ)
        yield
        pl.add("vector", lambda e: e.scalar_tensor_tensor(
            S3, S5, 0.5, S4, OP.mult, OP.add), inc=vQ)
        yield
        pl.add("vector", lambda e: e.tensor_scalar(
            S1, S3, -1.0, 2.0, OP.mult, OP.add), inc=vQ)
        yield
        tok_r = pl.add("vector", lambda e: e.tensor_tensor(
            S4, S2, S1, OP.mult), inc=vQ)
        yield

        def cast_fn(e):
            e.copy(I_, S4)
            e.drain()
            return e.copy(S1, I_)
        tok_cast = pl.add("scalar", cast_fn, waits=[(vQ, tok_r)],
                          inc=sQ)
        yield
        pl.add("vector", lambda e: e.tensor_tensor(
            S3, S4, S1, OP.subtract), waits=[(sQ, tok_cast)],
            inc=vQ)
        yield
        pl.add("vector", lambda e: e.tensor_single_scalar(
            S2, S3, 0.5, OP.is_gt), inc=vQ)
        yield
        pl.add("vector", lambda e: e.scalar_tensor_tensor(
            S2, S3, 1.5, S2, OP.is_gt, OP.add), inc=vQ)
        yield
        pl.add("vector", lambda e: e.scalar_tensor_tensor(
            S2, S3, 2.5, S2, OP.is_gt, OP.add), inc=vQ)
        yield
        pl.add("vector", lambda e: e.tensor_single_scalar(
            S4, S3, -0.5, OP.is_lt), inc=vQ)
        yield
        pl.add("vector", lambda e: e.scalar_tensor_tensor(
            S4, S3, -1.5, S4, OP.is_lt, OP.add), inc=vQ)
        yield
        pl.add("vector", lambda e: e.scalar_tensor_tensor(
            S4, S3, -2.5, S4, OP.is_lt, OP.add), inc=vQ)
        yield
        pl.add("vector", lambda e: e.tensor_tensor(
            S3, S2, S4, OP.subtract), inc=vQ)
        yield
        pl.add("vector", lambda e: e.tensor_tensor(
            S2, S1, S3, OP.add), inc=vQ)
        yield
        pl.add("vector", lambda e: e.tensor_single_scalar(
            S3, S6, 0.5, OP.add), inc=vQ)
        yield
        pl.add("vector", lambda e: e.tensor_tensor(
            S1, S2, S3, OP.mult), inc=vQ)
        yield
        box["tok"] = pl.add("vector", lambda e: e.tensor_tensor(
            v3(dst), v3(S1), bc(psc_t), OP.mult), inc=vQ)

    import os
    if os.environ.get("KSER"):
        m = {"dIN": dIN, "vQ": vQ, "sQ": sQ, "tF": tF, "gID": gID}
        pl.serialize = tuple(m[k] for k in os.environ["KSER"].split(","))
        pl.serialize_eng = os.environ.get("KSERENG")
    last_ft_evac = 0
    last_wq_reader = 0
    for oc in range(DC):
        tok_w = pl.add("sync", lambda e, oc=oc: e.dma_start(
            out=wfr[:, :], in_=w_f[oc * 128:(oc + 1) * 128, :]),
            waits=[(vQ, vQ.count)], inc=dIN, n=16)
        dws = dw_sb[:, oc * 32:(oc + 1) * 32]
        t0 = pl.add("scalar", lambda e, dws=dws: e.activation(
            wt_f[0][:, :], dws, AF.Exp, scale=-8.0),
            waits=[(dIN, tok_dw), (vQ, vQ.count)], inc=sQ)
        pl.add("vector", lambda e: e.tensor_scalar(
            wt_f[1][:, :], wt_f[0][:, :], -1.0, 1.0, OP.mult, OP.add),
            waits=[(sQ, t0)], inc=vQ)
        pl.add("vector", lambda e: e.tensor_scalar(
            wt_f[2][:, :], wt_f[0][:, :], 1.0, None, OP.add), inc=vQ)
        pl.add("vector", lambda e: _recip(
            e, wt_f[0][:, :], wt_f[2][:, :]), inc=vQ)
        tok_tw = pl.add("vector", lambda e: e.tensor_mul(
            wt_f[1][:, :], wt_f[1][:, :], wt_f[0][:, :]), inc=vQ)
        tok_cw = pl.add("scalar", lambda e: e.activation(
            cw_t[:, :], wt_f[1][:, :], AF.Exp, scale=0.5 * LN2C),
            waits=[(vQ, tok_tw)], inc=sQ)
        tok_rcw = pl.add("scalar", lambda e: e.activation(
            rcw_t[:, :], wt_f[1][:, :], AF.Exp, scale=-0.5 * LN2C), inc=sQ)
        pl.add("vector", lambda e: e.tensor_reduce(
            am_t.ap().unsqueeze(2),
            wfr.ap().rearrange("p (b k) -> p b k", k=32),
            AX.X, OP.max, apply_absolute_value=True),
            waits=[(dIN, tok_w)], inc=vQ)
        boxw = {}
        for _ in emit_block_exponent(am_t, cb_p2l, rcw_t, cw_t,
                                     [(sQ, tok_rcw)], boxw):
            pass
        boxq = {}
        for _ in emit_quant_core(wfr[:, :], wq_tm,
                                 [(vQ, boxw["tok_v"]), (tF, last_wq_reader)],
                                 boxq):
            pass
        tok_q = boxq["tok"]

        def tr_fn(e):
            for c in range(DC):
                inst = e.transpose(t_ps[:, c * 128:(c + 1) * 128],
                                   wq_tm[:, c * 128:(c + 1) * 128],
                                   ident[:, :])
            return inst
        if oc == 0:
            for nm, tile in (("wq0", wq_tm), ("e0", e_t[:, :]),
                             ("rsc0", rsc_t[:, :]), ("psc0", psc_t[:, :]),
                             ("am0", am_t[:, :]), ("cw0", cw_t[:, :])):
                if nm in taps:
                    pl.add("sync", lambda e, nm=nm, tile=tile: e.dma_start(
                        out=taps[nm], in_=tile), waits=[(vQ, tok_q)],
                        inc=dOUT, n=16)
        tok_tr = pl.add("tensor", tr_fn,
                        waits=[(vQ, tok_q), (sQ, last_ft_evac),
                               (gID, tok_ident)], inc=tF)
        last_wq_reader = tok_tr
        wv_view = wf_t.ap().rearrange("p (c o) -> p c o", c=DC)
        last_ft_evac = pl.add(
            "scalar", lambda e, oc=oc, wv=wv_view: e.copy(
                wv[:, :, oc * 128:(oc + 1) * 128],
                t_ps.ap().rearrange("p (c o) -> p c o", c=DC)),
            waits=[(tF, tok_tr)], inc=sQ)
    tok_wft = last_ft_evac
    pl.serialize = ()
    if "wqt" in taps:
        pl.add("sync", lambda e: e.dma_start(out=taps["wqt"],
                                             in_=wf_t[:, :]),
               waits=[(sQ, tok_wft)], inc=dOUT, n=16)

    last_vXE = 0
    prev_tr_group = 0
    prev_mm_group = [0, 0]
    fps_half_tok = [0, 0]
    fps_half_tok_s = [0, 0]
    last_stage_dma = {"kq": 0, "v": 0}
    tok_scr_k = [0] * NKT
    tok_scr_v = [0] * (S // 128)
    tok_scr_q = 0

    for gr in range(NG1):
        par = gr % 2
        own = gr < NOWN1
        rows = slice(gr * 512, (gr + 1) * 512)
        tok_x = pl.add("gpsimd", lambda e, rows=rows: e.dma_start(
            out=xtm.rearrange("p (s n) -> p s n", s=4),
            in_=x_in[rows, :].rearrange("(s p) n -> p s n", p=128)),
            waits=[(tXT, prev_tr_group), (dIN, tok_din), (dGIN, tok_gin)],
            inc=dX1, n=16)
        for sb in range(4):
            def xtr_fn(e, sb=sb):
                for c in range(DC):
                    inst = e.transpose(
                        t_ps[:, c * 128:(c + 1) * 128],
                        xtm[:, sb * D + c * 128: sb * D + (c + 1) * 128],
                        ident[:, :])
                return inst
            tok_tr = pl.add("tensor", xtr_fn,
                            waits=[(dX1, tok_x), (vXE, last_vXE),
                                   (sQ, tok_wft), (gID, tok_ident)],
                            inc=tXT)
            xv = xfm[par].rearrange("p (c s n) -> p c s n", c=DC, s=4)
            last_vXE = pl.add(
                "vector", lambda e, sb=sb, xv=xv: e.tensor_copy(
                    xv[:, :, sb, :],
                    t_ps.ap().rearrange("p (c n) -> p c n", c=DC)),
                waits=[(tXT, tok_tr), (tPJ, prev_mm_group[par])], inc=vXE)
        prev_tr_group = tXT.count
        tok_xfm = last_vXE

        def proj_qk(wi, stage, bias_col, is_q, kind):
            toks = [0, 0]
            for dc2 in range(DC):
                half = dc2 % 2
                def mm_fn(e, wi=wi, dc2=dc2, half=half, par=par):
                    for di in range(DC):
                        inst = e.matmul(
                            p1_ps[half],
                            w3[wi][:, di * D + dc2 * 128:
                                   di * D + (dc2 + 1) * 128],
                            xfm[par][:, di * 512:(di + 1) * 512],
                            start=(di == 0), stop=(di == DC - 1))
                    return inst
                tok_mm = pl.add("tensor", mm_fn,
                                waits=[(vXE, tok_xfm),
                                       (sPJ, fps_half_tok_s[half]),
                                       (vPJ, fps_half_tok[half])], inc=tPJ)
                fn_e = lambda e, dc2=dc2, half=half, bias_col=bias_col, \
                    is_q=is_q: e.activation(
                        stage[:, dc2 * 512:(dc2 + 1) * 512],
                        p1_ps[half],
                        AF.Identity, bias=bias_col[:, dc2:dc2 + 1],
                        scale=QSCALE if is_q else 1.0)
                fps_half_tok_s[half] = pl.add(
                    "scalar", fn_e,
                    waits=[(tPJ, tok_mm), (dSCR, last_stage_dma[kind]),
                           (vQ, tok_bq32)],
                    inc=sPJ)
                toks[half] = fps_half_tok_s[half]
            return toks[(DC - 1) % 2], max(toks)

        _, tok_kev = proj_qk(1, kqstage, bk_sb, False, "kq")
        jt, hf = gr // 2, gr % 2

        def kdma_fn(e, jt=jt, hf=hf):
            for dc2 in range(DC):
                inst = e.dma_start(
                    out=k_scr[jt, dc2, :, hf * 512:(hf + 1) * 512],
                    in_=kqstage[:, dc2 * 512:(dc2 + 1) * 512])
                if dc2 < DC - 1:
                    inst.then_inc(dSCR.h, 16)
            return inst
        tok_kdma = pl.add("sync", kdma_fn,
                          waits=[(sPJ, tok_kev), (dSCR, dSCR.count)],
                          inc=dSCR, n=16, extra=16 * (DC - 1))
        last_stage_dma["kq"] = tok_kdma

        if own:
            _, tok_qev = proj_qk(0, kqstage, bq32_sb, True, "kq")

            def qdma_fn(e, gr=gr):
                for dc2 in range(DC):
                    inst = e.dma_start(
                        out=q_scr[dc2, :, gr * 512:(gr + 1) * 512],
                        in_=kqstage[:, dc2 * 512:(dc2 + 1) * 512])
                    if dc2 < DC - 1:
                        inst.then_inc(dSCR.h, 16)
                return inst
            tok_qdma = pl.add("sync", qdma_fn,
                              waits=[(sPJ, tok_qev), (dSCR, dSCR.count)],
                              inc=dSCR, n=16, extra=16 * (DC - 1))

        for sb in range(4):
            for nh in range(2):
                half = (sb * 2 + nh) % 2
                def vmm_fn(e, sb=sb, nh=nh, half=half, par=par):
                    for di in range(DC):
                        e.matmul(
                            p1_ps[half],
                            xfm[par][:, di * 512 + sb * 128:
                                     di * 512 + (sb + 1) * 128],
                            w3[2][:, di * D + nh * 512:
                                  di * D + (nh + 1) * 512],
                            start=(di == 0), stop=False)
                    inst = e.matmul(
                        p1_ps[half],
                        ones_r[:, :],
                        bv_row[:, nh * 512:(nh + 1) * 512],
                        start=False, stop=True)
                    return inst
                tok_mm = pl.add("tensor", vmm_fn,
                                waits=[(vXE, tok_xfm),
                                       (sPJ, fps_half_tok_s[half]),
                                       (vPJ, fps_half_tok[half])], inc=tPJ)
                fps_half_tok_s[half] = pl.add(
                    "scalar",
                    lambda e, sb=sb, nh=nh, half=half: e.copy(
                        vstage[:, sb * D + nh * 512: sb * D + (nh + 1) * 512],
                        p1_ps[half]),
                    waits=[(tPJ, tok_mm), (dSCR, last_stage_dma["v"])],
                    inc=sPJ)

        def vdma_fn(e, gr=gr):
            for sb in range(4):
                inst = e.dma_start(
                    out=v_scr[gr * 4 + sb, :, :],
                    in_=vstage[:, sb * D:(sb + 1) * D])
                if sb < 3:
                    inst.then_inc(dSCR.h, 16)
            return inst
        tok_vdma = pl.add("sync", vdma_fn,
                          waits=[(sPJ, max(fps_half_tok_s)),
                                 (dSCR, last_stage_dma["kq"])],
                          inc=dSCR, n=16, extra=16 * 3)
        last_stage_dma["v"] = tok_vdma
        last_stage_dma["kq"] = tok_vdma
        for sb in range(4):
            tok_scr_v[gr * 4 + sb] = tok_vdma
        if own:
            tok_scr_q = tok_vdma
        if hf == 1:
            tok_scr_k[jt] = tok_vdma
        prev_mm_group[par] = tPJ.count

    tok_p1_t = tPJ.count
    tok_p1_tr = tXT.count
    tok_p1_tr_evac = vXE.count
    tok_p1_d = dSCR.count

    if "qq" in taps:
        pl.add("sync", lambda e: e.dma_start(out=taps["qq"], in_=q_scr),
               waits=[(dSCR, tok_p1_d)], inc=dOUT, n=16)
    if "kk" in taps:
        pl.add("sync", lambda e: e.dma_start(out=taps["kk"], in_=k_scr),
               waits=[(dSCR, tok_p1_d)], inc=dOUT, n=16)
    if "vv" in taps:
        pl.add("sync", lambda e: e.dma_start(out=taps["vv"], in_=v_scr),
               waits=[(dSCR, tok_p1_d)], inc=dOUT, n=16)

    tok_tS = {}
    tok_vR = {}
    tok_sE = {}
    tok_sA = {}
    tok_tT = {}
    tok_sTE = {}
    tok_tAV = {}
    tok_vO = {}
    pair_of = {}
    pair_idx = 0

    last_k_user = [0, 0]
    last_v_user = [0, 0]
    tok_kload = [0, 0]
    tok_vload = [0, 0]
    tok_qload = 0
    tok_linv = {}

    def emit_k_load(g, j):
        kp = j % 2

        def kl_fn(e, j=j):
            for dc in range(DC):
                inst = e.dma_start(out=k_sb[kp][:, dc * KT:(dc + 1) * KT],
                                   in_=k_scr[j, dc, :, :])
                if dc < DC - 1:
                    inst.then_inc(dKV.h, 16)
            return inst
        tok_kload[kp] = pl.add(
            "sync", kl_fn,
            waits=[(tS, last_k_user[kp]), (dSCR, tok_scr_k[j]),
                   (tPJ, tok_p1_t), (tXT, tok_p1_tr),
                   (dKV, dKV.count)],
            inc=dKV, n=16, extra=16 * (DC - 1))

    def emit_v_load(g, j):
        vp = j % 2

        def vl_fn(e, j=j):
            for c in range(KT // 128):
                inst = e.dma_start(out=v_sb[vp][:, c * D:(c + 1) * D],
                                   in_=v_scr[j * (KT // 128) + c, :, :])
                if c < KT // 128 - 1:
                    inst.then_inc(dKVV.h, 16)
            return inst
        if j >= 2:
            vu = pair_of[(g, j - 2, NT_G - 1)] + 1
        elif g > 0:
            vu = pair_of[(g - 1, NKT - 2 + j, NT_G - 1)] + 1
        else:
            vu = 0
        tok_vload[vp] = pl.add(
            "sync", vl_fn,
            waits=[(tAV, vu), (dSCR, tok_scr_v[(j + 1) * 8 - 1]),
                   (tPJ, tok_p1_t), (tXT, tok_p1_tr),
                   (dKVV, dKVV.count)],
            inc=dKVV, n=16, extra=16 * (KT // 128 - 1))

    def emit_q_load(g):
        nonlocal tok_qload

        def ql_fn(e, g=g):
            for dc in range(DC):
                inst = e.dma_start(
                    out=q_fm[:, dc * QG:(dc + 1) * QG],
                    in_=q_scr[dc, :, g * QG:(g + 1) * QG])
                if dc < DC - 1:
                    inst.then_inc(dQL.h, 16)
            return inst
        prev_user = tok_tS.get((g - 1, NKT - 1, NT_G - 1), 0)
        tok_qload = pl.add("sync", ql_fn,
                           waits=[(tS, prev_user), (dSCR, tok_scr_q),
                                  (tPJ, tok_p1_t), (tXT, tok_p1_tr),
                                  (dQL, dQL.count)],
                           inc=dQL, n=16, extra=16 * (DC - 1))

    def emit_scores(cur, prev):
        g, j, i = cur
        jp = j % 2
        w = [(dKV, tok_kload[jp]), (dQL, tok_qload)]
        if prev is not None:
            w.append((sE, tok_sE[prev]))
            w.append((vR, tok_vR[prev]))

        def sc_fn(e, g=g, j=j, i=i, jp=jp):
            for nh in range(2):
                for dc in range(DC):
                    inst = e.matmul(
                        s_ps[:, nh * 512:(nh + 1) * 512],
                        q_fm[:, dc * QG + i * 128: dc * QG + (i + 1) * 128],
                        k_sb[jp][:, dc * KT + nh * 512:
                                 dc * KT + (nh + 1) * 512],
                        start=(dc == 0), stop=(dc == DC - 1))
            return inst
        tok_tS[cur] = pl.add("tensor", sc_fn, waits=w, inc=tS)
        last_k_user[jp] = tok_tS[cur]

    def emit_vm(cur):
        g, j, i = cur
        if j != 0:
            tok_vR[cur] = pl.add(
                "gpsimd", lambda e: e.nop(),
                waits=[(tS, tok_tS[cur])], inc=vR)
            return
        tok_vR[cur] = pl.add(
            "vector", lambda e, i=i: e.reduce_max(
                negm_g[:, i:i + 1], s_ps[:, :], AX.X, negate=True),
            waits=[(tS, tok_tS[cur])], inc=vR)

    def emit_exp(cur):
        g, j, i = cur
        t = pair_of[cur]
        tp = t % 2

        def ex_fn(e, tp=tp, i=i, lp=t % 3):
            return e.activation(at_sb[tp], s_ps[:, :], AF.Exp,
                                bias=negm_g[:, i:i + 1],
                                accum_out=lt_b[lp][:, :])
        tok_sE[cur] = pl.add(
            "scalar", ex_fn,
            waits=[(vR, tok_vR[cur]), (tT, max(0, t - 1)),
                   (vO, max(0, t - 2))], inc=sE)

    def emit_transp(prev):
        t = pair_of[prev]
        tp = t % 2

        def tr_fn(e, tp=tp):
            for c in range(DC):
                inst = e.transpose(t_ps[:, c * 128:(c + 1) * 128],
                                   at_sb[tp][:, c * 128:(c + 1) * 128],
                                   ident[:, :])
            return inst
        w_t = [(sE, tok_sE[prev]), (sTE, max(0, t)),
               (vXE, tok_p1_tr_evac)]
        if tps_guard[0]:
            w_t.append((sQ, tps_guard[0]))
        tok_tT[prev] = pl.add("tensor", tr_fn, waits=w_t, inc=tT)

    def emit_evacT(prev):
        t = pair_of[prev]
        tp = t % 2
        tok_sTE[prev] = pl.add(
            "scalar", lambda e, tp=tp: e.copy(att_sb[tp], t_ps[:, :]),
            waits=[(tT, tok_tT[prev]), (tAV, max(0, t - 1))], inc=sTE)

    def emit_av(prev):
        g, j, i = prev
        t = pair_of[prev]
        tp = t % 2
        vp = j % 2

        def av_fn(e, tp=tp, vp=vp, op_=t % 2):
            for nh in range(2):
                for kc in range(KT // 128):
                    inst = e.matmul(
                        o_ps[op_][:, nh * 512:(nh + 1) * 512],
                        att_sb[tp][:, kc * 128:(kc + 1) * 128],
                        v_sb[vp][:, kc * D + nh * 512:
                                 kc * D + (nh + 1) * 512],
                        start=(kc == 0), stop=(kc == KT // 128 - 1))
            return inst
        tok_tAV[prev] = pl.add(
            "tensor", av_fn,
            waits=[(sTE, tok_sTE[prev]), (vO, max(0, t - 1)),
                   (dKVV, tok_vload[vp])], inc=tAV)
        last_v_user[vp] = tok_tAV[prev]

    def emit_vo(prev):
        g, j, i = prev
        t = pair_of[prev]
        tp = t % 2

        def vo_fn(e, g=g, j=j, i=i, lp=t % 3, op_=t % 2):
            lnew = l_cur[(j + 1) % 2][:, i:i + 1]
            osl = o_acc[g % 2][:, i * D:(i + 1) * D]
            if j == 0:
                e.tensor_copy(lnew, lt_b[lp][:, :])
                e.drain()
                inst = e.tensor_copy(osl, o_ps[op_][:, :])
            else:
                lold = l_cur[j % 2][:, i:i + 1]
                e.tensor_add(lnew, lold, lt_b[lp][:, :])
                e.drain()
                inst = e.scalar_tensor_tensor(osl, osl, 1.0,
                                              o_ps[op_][:, :],
                                              OP.mult, OP.add)
            return inst
        w = [(tAV, tok_tAV[prev]), (sE, tok_sE[prev])]
        tok_vO[prev] = pl.add("vector", vo_fn, waits=w, inc=vO)

    def emit_cycle(cur, prev1, prev2):
        if prev1 is not None:
            emit_transp(prev1)
            emit_evacT(prev1)
        if cur is not None:
            emit_scores(cur, prev1)
        if prev2 is not None:
            emit_av(prev2)
        if cur is not None:
            emit_vm(cur)
        if prev2 is not None:
            emit_vo(prev2)
        if cur is not None:
            emit_exp(cur)

    last_xres_user = 0
    last_out_user = 0
    last_ftps_user = 0
    last_fps_user = 0
    last_pa_reader = [0, 0]
    h_tap_tok = [0, 0]
    tps_guard = [0]
    post_tok = {}

    def ln_chain(src, dst, box, final_wait=()):
        pl.add("vector", lambda e: e.reduce_sum(
            st_f[0][:, :], src, AX.X), inc=vQ)
        yield
        tok_mean = pl.add("vector", lambda e: e.tensor_scalar(
            st_f[1][:, :], st_f[0][:, :], 1.0 / D, None, OP.mult), inc=vQ)
        yield
        tok_sq = pl.add("scalar", lambda e: e.activation(
            sq_dump[:, :], src, AF.Square, accum_out=st_f[2][:, :]),
            waits=[(vQ, tok_mean)], inc=sQ)
        yield
        pl.add("vector", lambda e: e.tensor_mul(
            st_f[4][:, :], st_f[1][:, :], st_f[1][:, :]), inc=vQ)
        yield
        tok_v = pl.add("vector", lambda e: e.scalar_tensor_tensor(
            st_f[3][:, :], st_f[2][:, :], 1.0 / D, st_f[4][:, :],
            OP.mult, OP.subtract), waits=[(sQ, tok_sq)], inc=vQ)
        yield
        tok_ln = pl.add("scalar", lambda e: e.activation(
            st_f[4][:, :], st_f[3][:, :], AF.Ln, bias=cb_eps[:, :]),
            waits=[(vQ, tok_v)], inc=sQ)
        yield
        tok_r0 = pl.add("scalar", lambda e: e.activation(
            st_f[5][:, :], st_f[4][:, :], AF.Exp, scale=-0.5),
            waits=[(sQ, tok_ln)], inc=sQ)
        yield
        box["tok"] = pl.add("vector", lambda e: e.tensor_scalar(
            dst, src, st_f[1][:, :], st_f[5][:, :], OP.subtract, OP.mult),
            waits=list(final_wait) + [(sQ, tok_r0)], inc=vQ)

    def emit_post_a(g, i):
        nonlocal last_xres_user
        hp = i % 2
        row0 = g * QG + i * 128
        while (g, NKT - 1, i) not in tok_vO or (g, i) not in tok_linv:
            yield "stall"
        tok_x = pl.add("gpsimd", lambda e, row0=row0: e.dma_start(
            out=x_res[:, :], in_=x_in[row0:row0 + 128, :]),
            waits=[(vQ, last_xres_user)], inc=dXR, n=16)
        yield
        tok_r = pl.add(
            "vector", lambda e, g=g, i=i: e.scalar_tensor_tensor(
                t2_sb[:, :], o_acc[g % 2][:, i * D:(i + 1) * D],
                linv_g[:, i:i + 1], x_res[:, :], OP.mult, OP.add),
            waits=[(dXR, tok_x), (vQ, tok_linv[(g, i)]),
                   (vO, tok_vO[(g, NKT - 1, i)])], inc=vQ)
        last_xres_user = tok_r
        yield
        box = {}
        fw = [(dTAP, dTAP.count)] if "h1" in taps else ()
        yield from ln_chain(t2_sb[:, :], h_sb[hp][:, :], box, final_wait=fw)
        tok_h = box["tok"]
        if "h1" in taps:
            h_tap_tok[hp] = pl.add(
                "gpsimd", lambda e, row0=row0, hp=hp: e.dma_start(
                    out=taps["h1"][row0:row0 + 128, :], in_=h_sb[hp][:, :]),
                waits=[(vQ, tok_h)], inc=dTAP, n=16)
            yield
        pl.add("vector", lambda e, hp=hp: e.tensor_reduce(
            am_t.ap().unsqueeze(2),
            h_sb[hp].ap().rearrange("p (b k) -> p b k", k=32),
            AX.X, OP.max, apply_absolute_value=True), inc=vQ)
        yield
        box2 = {}
        yield from emit_block_exponent(am_t, bias_act, None, None, [], box2)
        tok_sc = box2["tok_s"]
        box3 = {}
        yield from emit_quant_core(
            h_sb[hp][:, :], pa_sb[hp],
            [(sQ, tok_sc), (tF, last_pa_reader[hp])], box3)
        post_tok[("q", g, i)] = box3["tok"]

    def emit_post_b(g, i):
        nonlocal last_out_user, last_ftps_user, last_fps_user
        hp = i % 2
        row0 = g * QG + i * 128

        def ftr_fn(e, hp=hp):
            for c in range(DC):
                inst = e.transpose(t_ps[:, c * 128:(c + 1) * 128],
                                   pa_sb[hp][:, c * 128:(c + 1) * 128],
                                   ident[:, :])
            return inst
        tok_tr = pl.add("tensor", ftr_fn,
                        waits=[(vQ, post_tok[("q", g, i)]),
                               (sQ, last_ftps_user),
                               (sTE, sTE.count)], inc=tF)
        post_tok[("tr", g, i)] = tok_tr
        last_pa_reader[hp] = tok_tr
        tok_ev = pl.add("scalar", lambda e: e.copy(
            xqt_sb, t_ps[:, :]), waits=[(tF, tok_tr)], inc=sQ)
        last_ftps_user = tok_ev
        tps_guard[0] = tok_ev
        yield
        for nh in range(2):
            def fmm_fn(e, nh=nh):
                for dc in range(DC):
                    inst = e.matmul(
                        f_ps[:, :],
                        xqt_sb[:, dc * 128:(dc + 1) * 128],
                        wf_t[:, dc * D + nh * 512: dc * D + (nh + 1) * 512],
                        start=(dc == 0), stop=(dc == DC - 1))
                return inst
            tok_fm = pl.add("tensor", fmm_fn,
                            waits=[(sQ, tok_ev), (vQ, last_fps_user)],
                            inc=tF)
            yield
            last_fps_user = pl.add(
                "vector", lambda e, nh=nh: e.scalar_tensor_tensor(
                    t2_sb[:, nh * 512:(nh + 1) * 512], f_ps[:, :],
                    c_a[:, :], bf_sb[:, nh * 512:(nh + 1) * 512],
                    OP.mult, OP.add),
                waits=[(tF, tok_fm), (sQ, tok_ca)], inc=vQ)
            yield
        pl.add("vector", lambda e, hp=hp: e.tensor_tensor(
            qs_f[0][:, :], t2_sb[:, :], h_sb[hp][:, :], OP.add), inc=vQ)
        yield
        box = {}
        yield from ln_chain(qs_f[0][:, :], out_sb[:, :], box,
                            final_wait=[(dOUT, last_out_user)])
        tok_out = box["tok"]
        last_out_user = pl.add(
            "gpsimd", lambda e, row0=row0: e.dma_start(
                out=out[row0:row0 + 128, :], in_=out_sb[:, :]),
            waits=[(vQ, tok_out)], inc=dOUT, n=16)

    def post_group_gen(g):
        yield from emit_post_a(g, 0)
        yield from emit_post_a(g, 1)
        yield from emit_post_b(g, 0)
        yield from emit_post_a(g, 2)
        yield from emit_post_b(g, 1)
        yield from emit_post_a(g, 3)
        yield from emit_post_b(g, 2)
        yield from emit_post_b(g, 3)

    for g in range(NQG):
        for j in range(NKT):
            for i in range(NT_G):
                pair_of[(g, j, i)] = pair_idx
                pair_idx += 1

    PACE = 13
    post_gen = [None]

    def dispense(n):
        if post_gen[0] is None:
            return
        for _ in range(n):
            r = next(post_gen[0], "END")
            if r == "END":
                post_gen[0] = None
                return
            if r == "stall":
                return

    def after_vo(p2):
        if p2 is None or p2[1] != NKT - 1:
            return
        fg, i = p2[0], p2[2]
        tok_linv[(fg, i)] = pl.add(
            "vector", lambda e, i=i: _recip(e, linv_g[:, i:i + 1],
                                            l_cur[NKT % 2][:, i:i + 1]),
            waits=[(vO, tok_vO[p2])], inc=vQ)
        if i == 0:
            while post_gen[0] is not None:
                dispense(10 ** 9)
            post_gen[0] = post_group_gen(fg)

    prev1 = None
    prev2 = None
    for g in range(NQG):
        emit_q_load(g)
        if g == 0:
            emit_k_load(0, 0)
            emit_v_load(0, 0)
        for j in range(NKT):
            for i in range(NT_G):
                cur = (g, j, i)
                emit_cycle(cur, prev1, prev2)
                after_vo(prev2)
                prev2 = prev1
                prev1 = cur
                dispense(PACE)
            if j + 1 < NKT:
                emit_k_load(g, j + 1)
                emit_v_load(g, j + 1)
            elif g + 1 < NQG:
                emit_k_load(g + 1, 0)
                emit_v_load(g + 1, 0)

    emit_cycle(None, prev1, prev2)
    after_vo(prev2)
    emit_cycle(None, None, prev1)
    after_vo(prev1)
    while post_gen[0] is not None:
        dispense(10 ** 9)

    pl.add("sync", lambda e: e.nop(), waits=[(dOUT, dOUT.count)], inc=None)

    with nc.Block() as block:
        pl.emit(block)
    return nc


_NC_CACHE = {}


def _get_program(debug_taps=()):
    key = tuple(sorted(debug_taps))
    if key not in _NC_CACHE:
        _NC_CACHE[key] = build_program(debug_taps=key)
    return _NC_CACHE[key]


def kernel(L0_attn_X, L0_attn_W_q, L0_attn_W_k, L0_attn_W_v, L0_attn_b_q,
           L0_attn_b_k, L0_attn_b_v, L0_ffn_weight, L0_ffn_bias,
           L0_ffn_delta_w, L0_ffn_delta_act, _debug_taps=(), _cores=None):
    X = np.asarray(L0_attn_X, dtype=np.float32)
    nc = _get_program(_debug_taps)
    cores = list(range(NCORES)) if _cores is None else _cores
    in_maps = []
    for c in cores:
        b, h = c // 2, c % 2
        xb = X[b]
        if h == 1:
            xb = np.roll(xb, -OWN, axis=0)
        in_maps.append({
            "x_in": np.ascontiguousarray(xb),
            "w_q": np.ascontiguousarray(L0_attn_W_q, dtype=np.float32),
            "w_k": np.ascontiguousarray(L0_attn_W_k, dtype=np.float32),
            "w_v": np.ascontiguousarray(L0_attn_W_v, dtype=np.float32),
            "b_q": np.ascontiguousarray(L0_attn_b_q, dtype=np.float32),
            "b_k": np.ascontiguousarray(L0_attn_b_k, dtype=np.float32),
            "b_v": np.ascontiguousarray(L0_attn_b_v, dtype=np.float32),
            "w_f": np.ascontiguousarray(L0_ffn_weight, dtype=np.float32),
            "b_f": np.ascontiguousarray(L0_ffn_bias, dtype=np.float32),
            "d_w": np.ascontiguousarray(L0_ffn_delta_w, dtype=np.float32),
            "d_a": np.ascontiguousarray(L0_ffn_delta_act, dtype=np.float32),
        })
    res = run_bass_kernel_spmd(nc, in_maps, core_ids=list(range(len(cores))))
    if _debug_taps or _cores is not None:
        return res
    outp = np.empty((4, S, D), dtype=np.float32)
    for idx, c in enumerate(cores):
        b, h = c // 2, c % 2
        outp[b, h * OWN:(h + 1) * OWN] = np.asarray(res.results[idx]["out"])
    return outp


# version 23 (final)
# speedup vs baseline: 1.0573x; 1.0573x over previous
import numpy as np
import concourse.bass as bass
import concourse.mybir as mybir
from concourse.bass_utils import run_bass_kernel_spmd

F32 = mybir.dt.float32
BF16 = mybir.dt.bfloat16
I32 = mybir.dt.int32
AX = mybir.AxisListType
AF = mybir.ActivationFunctionType
OP = mybir.AluOpType

NCORES = 8
S = 8192
D = 1024
OWN = 4096
QG = 512
NQG = OWN // QG
KT = 1024
NKT = S // KT
DC = D // 128
NT_G = QG // 128
LN2C = float(np.log(2.0))
EPS = 1e-5
TINY = 1e-30
QSCALE = 1.0 / 32.0
NG1 = S // 512
NOWN1 = OWN // 512


class Sem:
    def __init__(self, nc, name):
        self.h = nc.alloc_semaphore(name)
        self.count = 0

    def inc(self, n=1):
        self.count += n
        return self.count


class Plan:
    def __init__(self):
        self.steps = {k: [] for k in ("sync", "gpsimd", "vector", "scalar",
                                      "tensor")}

    serialize = ()
    serialize_eng = None

    def add(self, eng, fn, waits=(), inc=None, n=1, extra=0):
        if self.serialize and (self.serialize_eng is None
                               or (eng == self.serialize_eng[1:]
                                   if self.serialize_eng[0] == "+"
                                   else eng != self.serialize_eng[1:])):
            waits = list(waits) + [(s, s.count) for s in self.serialize
                                   if s.count > 0]
        tok = inc.inc(n + extra) if inc is not None else None
        self.steps[eng].append((tuple(waits), fn, inc, n))
        return tok

    def emit(self, block):
        def mk(name):
            sep = name == "vector"

            def run(e):
                first = True
                for waits, fn, inc, n in self.steps[name]:
                    if sep and not first:
                        e.drain()
                    first = False
                    for sem, val in waits:
                        if val and val > 0:
                            e.wait_ge(sem.h, val)
                    inst = fn(e)
                    if inc is not None:
                        inst.then_inc(inc.h, n)
            return run
        block.sync(mk("sync"))
        block.gpsimd(mk("gpsimd"))
        block.vector(mk("vector"))
        block.scalar(mk("scalar"))
        block.tensor(mk("tensor"))


def build_program(debug_taps=()):
    nc = bass.Bass("TRN2", target_bir_lowering=False)

    x_in = nc.dram_tensor("x_in", [S, D], F32, kind="ExternalInput").ap()
    w_q = nc.dram_tensor("w_q", [D, D], F32, kind="ExternalInput").ap()
    w_k = nc.dram_tensor("w_k", [D, D], F32, kind="ExternalInput").ap()
    w_v = nc.dram_tensor("w_v", [D, D], F32, kind="ExternalInput").ap()
    b_q = nc.dram_tensor("b_q", [D], F32, kind="ExternalInput").ap()
    b_k = nc.dram_tensor("b_k", [D], F32, kind="ExternalInput").ap()
    b_v = nc.dram_tensor("b_v", [D], F32, kind="ExternalInput").ap()
    w_f = nc.dram_tensor("w_f", [D, D], F32, kind="ExternalInput").ap()
    b_f = nc.dram_tensor("b_f", [D], F32, kind="ExternalInput").ap()
    d_w = nc.dram_tensor("d_w", [D, 32], F32, kind="ExternalInput").ap()
    d_a = nc.dram_tensor("d_a", [1], F32, kind="ExternalInput").ap()
    out = nc.dram_tensor("out", [OWN, D], F32, kind="ExternalOutput").ap()

    q_scr = nc.dram_tensor("q_scr", [DC, 128, OWN], BF16).ap()
    k_scr = nc.dram_tensor("k_scr", [NKT, DC, 128, KT], BF16).ap()
    v_scr = nc.dram_tensor("v_scr", [S // 128, 128, D], BF16).ap()

    taps = {}
    for name in debug_taps:
        shp = {"h1": [OWN, D], "qq": [DC, 128, OWN], "kk": [NKT, DC, 128, KT],
               "vv": [S // 128, 128, D], "wqt": [128, DC * D],
               "wq0": [128, D], "e0": [128, 32], "rsc0": [128, 32],
               "psc0": [128, 32], "am0": [128, 32], "cw0": [128, 32]}[name]
        dt = BF16 if name in ("qq", "kk", "vv", "wqt", "wq0") else F32
        taps[name] = nc.dram_tensor("tap_" + name, shp, dt,
                                    kind="ExternalOutput").ap()

    A = nc.alloc_sbuf_tensor

    AR = A("arena", [128, 46080], BF16)

    def arv(off, n):
        return AR[:, off:off + n]

    w3 = [arv(i * 8192, 8192) for i in range(3)]
    xtm = arv(24576, 4096)
    xfm = [arv(28672 + p * 4096, 4096) for p in range(2)]
    kqstage = arv(36864, 4096)
    vstage = arv(40960, 4096)
    wq_tm = arv(45056, 1024)
    q_fm = arv(0, DC * QG)
    k_sb = [arv(4096 + p * 8192, 8192) for p in range(2)]
    v_sb = [arv(20480 + p * 8192, 8192) for p in range(2)]
    at_sb = [arv(36864 + p * 1024, 1024) for p in range(2)]
    att_sb = [arv(38912 + p * 1024, 1024) for p in range(2)]
    pa_sb = [arv(40960 + p * 1024, 1024) for p in range(2)]
    xqt_sb = arv(43008, 1024)

    ident = A("ident", [128, 128], BF16)
    bq_sb = A("bq_sb", [128, DC], F32)
    bq32_sb = A("bq32_sb", [128, DC], F32)
    bk_sb = A("bk_sb", [128, DC], F32)
    bf_sb = A("bf_sb", [128, D], F32)
    ones_r = A("ones_r", [1, 128], BF16)
    bv_row = A("bv_row", [1, D], BF16)
    da_sb = A("da_sb", [128, 1], F32)
    c_a = A("c_a", [128, 1], F32)
    bias_act = A("bias_act", [128, 1], F32)
    cb_p2l = A("cb_p2l", [128, 1], F32)
    cb_n2l = A("cb_n2l", [128, 1], F32)
    cb_eps = A("cb_eps", [128, 1], F32)

    wf_t = A("wf_t", [128, DC * D], BF16)
    wfr = A("wfr", [128, D], F32)
    dw_sb = A("dw_sb", [128, DC * 32], F32)

    qs_f = [A(f"qs_f{i}", [128, D], F32) for i in range(6)]
    qs_i = A("qs_i", [128, D], I32)
    sq_dump = A("sq_dump", [128, D], F32)
    be_f = [A(f"be_f{i}", [128, 32], F32) for i in range(4)]
    be_i = [A(f"be_i{i}", [128, 32], I32) for i in range(2)]
    wt_f = [A(f"wt_f{i}", [128, 32], F32) for i in range(3)]
    cw_t = A("cw_t", [128, 32], F32)
    rcw_t = A("rcw_t", [128, 32], F32)
    am_t = A("am_t", [128, 32], F32)
    e_t = A("e_t", [128, 32], F32)
    rsc_t = A("rsc_t", [128, 32], F32)
    psc_t = A("psc_t", [128, 32], F32)

    o_acc = [A(f"o_acc{p}", [128, NT_G * D], F32) for p in range(2)]
    l_cur = [A(f"l_cur{p}", [128, NT_G], F32) for p in range(2)]
    negm_g = A("negm_g", [128, NT_G], F32)
    linv_g = A("linv_g", [128, NT_G], F32)
    lt_b = [A(f"lt_b{p}", [128, 1], F32) for p in range(3)]
    lt_g = A("lt_g", [128, 1], F32)

    x_res = A("x_res", [128, D], F32)
    h_sb = [A(f"h_sb{p}", [128, D], F32) for p in range(2)]
    t2_sb = A("t2_sb", [128, D], F32)
    out_sb = A("out_sb", [128, D], F32)
    st_f = [A(f"st_f{i}", [128, 1], F32) for i in range(6)]

    AP_ = nc.alloc_psum_tensor
    s_ps = AP_("s_ps", [128, KT], F32)
    t_ps = AP_("t_ps", [128, KT], BF16)
    o_ps = [AP_(f"o_ps{p}", [128, D], F32) for p in range(2)]
    f_ps = AP_("f_ps", [128, 512], F32)
    p1_ps = [o_ps[0][:, 0:512], o_ps[0][:, 512:1024]]

    dIN = Sem(nc, "dIN")
    dGIN = Sem(nc, "dGIN")
    dX1 = Sem(nc, "dX1")
    dSCR = Sem(nc, "dSCR")
    dKV = Sem(nc, "dKV")
    dKVV = Sem(nc, "dKVV")
    dQL = Sem(nc, "dQL")
    dXR = Sem(nc, "dXR")
    dOUT = Sem(nc, "dOUT")
    dTAP = Sem(nc, "dTAP") if debug_taps else None
    tXT = Sem(nc, "tXT")
    vXE = Sem(nc, "vXE")
    tPJ = Sem(nc, "tPJ")
    vPJ = Sem(nc, "vPJ")
    sPJ = Sem(nc, "sPJ")
    tS = Sem(nc, "tS")
    vR = Sem(nc, "vR")
    sE = Sem(nc, "sE")
    sA = Sem(nc, "sA")
    tT = Sem(nc, "tT")
    sTE = Sem(nc, "sTE")
    tAV = Sem(nc, "tAV")
    vO = Sem(nc, "vO")
    vQ = Sem(nc, "vQ")
    sQ = Sem(nc, "sQ")
    tF = Sem(nc, "tF")
    gID = Sem(nc, "gID")

    pl = Plan()

    t_id0 = pl.add("gpsimd", lambda e: e.memset(ident[:, :], 0.0), inc=gID)
    tok_ident = pl.add("gpsimd", lambda e: e.affine_select(
        out=ident[:, :], in_=ident[:, :], compare_op=OP.not_equal, fill=1.0,
        base=0, pattern=[[-1, 128]], channel_multiplier=1),
        waits=[(gID, t_id0)], inc=gID)

    for src, dst in ((b_q, bq_sb), (b_k, bk_sb)):
        pl.add("sync", lambda e, s=src, d=dst: e.dma_start(
            out=d[:, :], in_=s.rearrange("(c p) -> p c", p=128),
            allow_slow_non_contiguous=True),
            inc=dIN, n=16)
    pl.add("gpsimd", lambda e: e.dma_start(
        out=bv_row[:, :], in_=b_v.unsqueeze(0)), inc=dGIN, n=16)
    pl.add("gpsimd", lambda e: e.memset(ones_r[:, :], 1.0), inc=None)
    pl.add("sync", lambda e: e.dma_start(
        out=bf_sb[:, :],
        in_=b_f.unsqueeze(0).broadcast_to([128, D])), inc=dIN, n=16)
    pl.add("sync", lambda e: e.dma_start(
        out=da_sb[:, :],
        in_=d_a.unsqueeze(0).broadcast_to([128, 1])), inc=dIN, n=16)
    for i, w in enumerate((w_q, w_k, w_v)):
        pl.add("gpsimd", lambda e, w=w, i=i: e.dma_start(
            out=w3[i].rearrange("p (c o) -> p c o", c=DC),
            in_=w.rearrange("(c p) o -> p c o", p=128)), inc=dGIN, n=16)
    tok_dw = pl.add("sync", lambda e: e.dma_start(
        out=dw_sb.ap().rearrange("p (c b) -> p c b", c=DC),
        in_=d_w.rearrange("(c p) b -> p c b", p=128)), inc=dIN, n=16)
    tok_din = dIN.count
    tok_gin = dGIN.count
    tok_bq32 = pl.add("vector", lambda e: e.tensor_single_scalar(
        bq32_sb[:, :], bq_sb[:, :], QSCALE, OP.mult),
        waits=[(dIN, tok_din)], inc=vQ)

    pl.add("vector", lambda e: e.memset(cb_p2l[:, :], 1.0), inc=vQ)
    pl.add("vector", lambda e: e.tensor_single_scalar(
        cb_p2l[:, :], cb_p2l[:, :], 2.0 * LN2C, OP.mult), inc=vQ)
    pl.add("vector", lambda e: e.memset(cb_n2l[:, :], 1.0), inc=vQ)
    pl.add("vector", lambda e: e.tensor_single_scalar(
        cb_n2l[:, :], cb_n2l[:, :], -2.0 * LN2C, OP.mult), inc=vQ)
    pl.add("vector", lambda e: e.memset(cb_eps[:, :], EPS), inc=vQ)
    tok_bq32 = None

    tok = pl.add("scalar", lambda e: e.activation(
        st_f[0][:, :], da_sb[:, :], AF.Exp, scale=-8.0),
        waits=[(dIN, tok_din)], inc=sQ)
    pl.add("vector", lambda e: e.tensor_scalar(
        st_f[1][:, :], st_f[0][:, :], -1.0, 1.0, OP.mult, OP.add),
        waits=[(sQ, tok)], inc=vQ)
    pl.add("vector", lambda e: e.tensor_scalar(
        st_f[2][:, :], st_f[0][:, :], 1.0, None, OP.add), inc=vQ)
    def _recip(e, out, in_):
        inst = e.reciprocal(out, in_)
        e.drain()
        return inst

    pl.add("vector", lambda e: _recip(e, st_f[3][:, :], st_f[2][:, :]),
           inc=vQ)
    tok_tanh = pl.add("vector", lambda e: e.tensor_mul(
        st_f[4][:, :], st_f[1][:, :], st_f[3][:, :]), inc=vQ)
    tok_ca = pl.add("scalar", lambda e: e.activation(
        c_a[:, :], st_f[4][:, :], AF.Exp, scale=0.5 * LN2C),
        waits=[(vQ, tok_tanh)], inc=sQ)
    pl.add("vector", lambda e: e.tensor_scalar(
        bias_act[:, :], st_f[4][:, :], -0.5 * LN2C, 2.0 * LN2C,
        OP.mult, OP.add), inc=vQ)

    def emit_block_exponent(src_am, rsc_bias, extra_rsc, extra_psc, wait_s,
                            box):
        b0, b1, b2, b3 = (t[:, :] for t in be_f)
        i0, i1 = (t[:, :] for t in be_i)
        pl.add("vector", lambda e: e.tensor_single_scalar(
            b0, src_am[:, :], TINY, OP.max), inc=vQ)
        yield
        pl.add("vector", lambda e: e.tensor_single_scalar(
            i0, be_f[0].ap().bitcast(I32), 0x7F800000, OP.bitwise_and),
            inc=vQ)
        yield
        pl.add("vector", lambda e: e.tensor_copy(
            b1, be_i[0].ap().bitcast(F32)), inc=vQ)
        yield
        pl.add("vector", lambda e: e.tensor_single_scalar(
            i1, i0, 23, OP.logical_shift_right), inc=vQ)
        yield
        pl.add("vector", lambda e: e.tensor_single_scalar(
            i0, i1, 127, OP.subtract), inc=vQ)
        yield
        pl.add("vector", lambda e: e.tensor_copy(b2, i0), inc=vQ)
        yield
        pl.add("vector", lambda e: e.tensor_single_scalar(
            b3, b1, 2.0, OP.mult), inc=vQ)
        yield
        pl.add("vector", lambda e: e.tensor_tensor(
            b3, b0, b3, OP.is_ge), inc=vQ)
        yield
        pl.add("vector", lambda e: e.tensor_tensor(
            b1, b0, b1, OP.is_lt), inc=vQ)
        yield
        pl.add("vector", lambda e: e.scalar_tensor_tensor(
            b0, b1, -1.0, b3, OP.mult, OP.add), inc=vQ)
        yield
        tok_e = pl.add("vector", lambda e: e.tensor_tensor(
            e_t[:, :], b2, b0, OP.add), inc=vQ)
        yield
        bias = rsc_bias[:, :]
        t1 = pl.add("scalar", lambda e, bias=bias: e.activation(
            rsc_t[:, :], e_t[:, :], AF.Exp, scale=-LN2C, bias=bias),
            waits=[(vQ, tok_e)] + list(wait_s), inc=sQ)
        yield
        t2 = pl.add("scalar", lambda e: e.activation(
            psc_t[:, :], e_t[:, :], AF.Exp, scale=LN2C, bias=cb_n2l[:, :]),
            inc=sQ)
        yield
        box["tok_s"] = t2
        box["tok_v"] = None
        if extra_rsc is not None:
            pl.add("vector", lambda e: e.tensor_mul(
                be_f[0][:, :], rsc_t[:, :], extra_rsc[:, :]),
                waits=[(sQ, t1)], inc=vQ)
            yield
            pl.add("vector", lambda e: e.tensor_copy(
                rsc_t[:, :], be_f[0][:, :]), inc=vQ)
            yield
            pl.add("vector", lambda e: e.tensor_mul(
                be_f[0][:, :], psc_t[:, :], extra_psc[:, :]),
                waits=[(sQ, t2)], inc=vQ)
            yield
            box["tok_v"] = pl.add("vector", lambda e: e.tensor_copy(
                psc_t[:, :], be_f[0][:, :]), inc=vQ)
            yield

    def emit_quant_core(src, dst, wait_first, box):
        S1, S2, S3, S4, S5, S6 = (t[:, :] for t in qs_f)
        I_ = qs_i[:, :]
        v3 = lambda t: t.rearrange("p (b k) -> p b k", k=32)
        bc = lambda t: t.ap().unsqueeze(2).broadcast_to([128, 32, 32])
        pl.add("vector", lambda e: e.tensor_tensor(
            v3(S1), v3(src), bc(rsc_t), OP.mult), waits=wait_first, inc=vQ)
        yield
        tok_clip = pl.add("vector", lambda e: e.tensor_scalar(
            S2, S1, 6.0, -6.0, OP.min, OP.max), inc=vQ)
        yield
        tok_aa = pl.add("scalar", lambda e: e.activation(
            qs_f[0][:, :], qs_f[1][:, :], AF.Abs),
            waits=[(vQ, tok_clip)], inc=sQ)
        yield
        pl.add("vector", lambda e: e.tensor_single_scalar(
            S4, S1, 2.0, OP.is_ge), waits=[(sQ, tok_aa)], inc=vQ)
        yield
        pl.add("vector", lambda e: e.tensor_single_scalar(
            S5, S1, 4.0, OP.is_ge), inc=vQ)
        yield
        pl.add("vector", lambda e: e.scalar_tensor_tensor(
            S6, S4, 0.5, S5, OP.mult, OP.add), inc=vQ)
        yield
        pl.add("vector", lambda e: e.scalar_tensor_tensor(
            S3, S5, 0.5, S4, OP.mult, OP.add), inc=vQ)
        yield
        pl.add("vector", lambda e: e.tensor_scalar(
            S1, S3, -1.0, 2.0, OP.mult, OP.add), inc=vQ)
        yield
        tok_r = pl.add("vector", lambda e: e.tensor_tensor(
            S4, S2, S1, OP.mult), inc=vQ)
        yield

        def cast_fn(e):
            e.copy(I_, S4)
            e.drain()
            return e.copy(S1, I_)
        tok_cast = pl.add("scalar", cast_fn, waits=[(vQ, tok_r)],
                          inc=sQ)
        yield
        pl.add("vector", lambda e: e.tensor_tensor(
            S3, S4, S1, OP.subtract), waits=[(sQ, tok_cast)],
            inc=vQ)
        yield
        pl.add("vector", lambda e: e.tensor_single_scalar(
            S2, S3, 0.5, OP.is_gt), inc=vQ)
        yield
        pl.add("vector", lambda e: e.scalar_tensor_tensor(
            S2, S3, 1.5, S2, OP.is_gt, OP.add), inc=vQ)
        yield
        pl.add("vector", lambda e: e.scalar_tensor_tensor(
            S2, S3, 2.5, S2, OP.is_gt, OP.add), inc=vQ)
        yield
        pl.add("vector", lambda e: e.tensor_single_scalar(
            S4, S3, -0.5, OP.is_lt), inc=vQ)
        yield
        pl.add("vector", lambda e: e.scalar_tensor_tensor(
            S4, S3, -1.5, S4, OP.is_lt, OP.add), inc=vQ)
        yield
        pl.add("vector", lambda e: e.scalar_tensor_tensor(
            S4, S3, -2.5, S4, OP.is_lt, OP.add), inc=vQ)
        yield
        pl.add("vector", lambda e: e.tensor_tensor(
            S3, S2, S4, OP.subtract), inc=vQ)
        yield
        pl.add("vector", lambda e: e.tensor_tensor(
            S2, S1, S3, OP.add), inc=vQ)
        yield
        pl.add("vector", lambda e: e.tensor_single_scalar(
            S3, S6, 0.5, OP.add), inc=vQ)
        yield
        pl.add("vector", lambda e: e.tensor_tensor(
            S1, S2, S3, OP.mult), inc=vQ)
        yield
        box["tok"] = pl.add("vector", lambda e: e.tensor_tensor(
            v3(dst), v3(S1), bc(psc_t), OP.mult), inc=vQ)

    import os
    if os.environ.get("KSER"):
        m = {"dIN": dIN, "vQ": vQ, "sQ": sQ, "tF": tF, "gID": gID}
        pl.serialize = tuple(m[k] for k in os.environ["KSER"].split(","))
        pl.serialize_eng = os.environ.get("KSERENG")
    last_ft_evac = 0
    last_wq_reader = 0
    for oc in range(DC):
        tok_w = pl.add("sync", lambda e, oc=oc: e.dma_start(
            out=wfr[:, :], in_=w_f[oc * 128:(oc + 1) * 128, :]),
            waits=[(vQ, vQ.count)], inc=dIN, n=16)
        dws = dw_sb[:, oc * 32:(oc + 1) * 32]
        t0 = pl.add("scalar", lambda e, dws=dws: e.activation(
            wt_f[0][:, :], dws, AF.Exp, scale=-8.0),
            waits=[(dIN, tok_dw), (vQ, vQ.count)], inc=sQ)
        pl.add("vector", lambda e: e.tensor_scalar(
            wt_f[1][:, :], wt_f[0][:, :], -1.0, 1.0, OP.mult, OP.add),
            waits=[(sQ, t0)], inc=vQ)
        pl.add("vector", lambda e: e.tensor_scalar(
            wt_f[2][:, :], wt_f[0][:, :], 1.0, None, OP.add), inc=vQ)
        pl.add("vector", lambda e: _recip(
            e, wt_f[0][:, :], wt_f[2][:, :]), inc=vQ)
        tok_tw = pl.add("vector", lambda e: e.tensor_mul(
            wt_f[1][:, :], wt_f[1][:, :], wt_f[0][:, :]), inc=vQ)
        tok_cw = pl.add("scalar", lambda e: e.activation(
            cw_t[:, :], wt_f[1][:, :], AF.Exp, scale=0.5 * LN2C),
            waits=[(vQ, tok_tw)], inc=sQ)
        tok_rcw = pl.add("scalar", lambda e: e.activation(
            rcw_t[:, :], wt_f[1][:, :], AF.Exp, scale=-0.5 * LN2C), inc=sQ)
        pl.add("vector", lambda e: e.tensor_reduce(
            am_t.ap().unsqueeze(2),
            wfr.ap().rearrange("p (b k) -> p b k", k=32),
            AX.X, OP.max, apply_absolute_value=True),
            waits=[(dIN, tok_w)], inc=vQ)
        boxw = {}
        for _ in emit_block_exponent(am_t, cb_p2l, rcw_t, cw_t,
                                     [(sQ, tok_rcw)], boxw):
            pass
        boxq = {}
        for _ in emit_quant_core(wfr[:, :], wq_tm,
                                 [(vQ, boxw["tok_v"]), (tF, last_wq_reader)],
                                 boxq):
            pass
        tok_q = boxq["tok"]

        def tr_fn(e):
            for c in range(DC):
                inst = e.transpose(t_ps[:, c * 128:(c + 1) * 128],
                                   wq_tm[:, c * 128:(c + 1) * 128],
                                   ident[:, :])
            return inst
        if oc == 0:
            for nm, tile in (("wq0", wq_tm), ("e0", e_t[:, :]),
                             ("rsc0", rsc_t[:, :]), ("psc0", psc_t[:, :]),
                             ("am0", am_t[:, :]), ("cw0", cw_t[:, :])):
                if nm in taps:
                    pl.add("sync", lambda e, nm=nm, tile=tile: e.dma_start(
                        out=taps[nm], in_=tile), waits=[(vQ, tok_q)],
                        inc=dOUT, n=16)
        tok_tr = pl.add("tensor", tr_fn,
                        waits=[(vQ, tok_q), (sQ, last_ft_evac),
                               (gID, tok_ident)], inc=tF)
        last_wq_reader = tok_tr
        wv_view = wf_t.ap().rearrange("p (c o) -> p c o", c=DC)
        last_ft_evac = pl.add(
            "scalar", lambda e, oc=oc, wv=wv_view: e.copy(
                wv[:, :, oc * 128:(oc + 1) * 128],
                t_ps.ap().rearrange("p (c o) -> p c o", c=DC)),
            waits=[(tF, tok_tr)], inc=sQ)
    tok_wft = last_ft_evac
    pl.serialize = ()
    if "wqt" in taps:
        pl.add("sync", lambda e: e.dma_start(out=taps["wqt"],
                                             in_=wf_t[:, :]),
               waits=[(sQ, tok_wft)], inc=dOUT, n=16)

    last_vXE = 0
    prev_tr_group = 0
    prev_mm_group = [0, 0]
    fps_half_tok = [0, 0]
    fps_half_tok_s = [0, 0]
    last_stage_dma = {"kq": 0, "v": 0}
    tok_scr_k = [0] * NKT
    tok_scr_v = [0] * (S // 128)
    tok_scr_q = 0

    for gr in range(NG1):
        par = gr % 2
        own = gr < NOWN1
        rows = slice(gr * 512, (gr + 1) * 512)
        tok_x = pl.add("gpsimd", lambda e, rows=rows: e.dma_start(
            out=xtm.rearrange("p (s n) -> p s n", s=4),
            in_=x_in[rows, :].rearrange("(s p) n -> p s n", p=128)),
            waits=[(tXT, prev_tr_group), (dIN, tok_din), (dGIN, tok_gin)],
            inc=dX1, n=16)
        for sb in range(4):
            def xtr_fn(e, sb=sb):
                for c in range(DC):
                    inst = e.transpose(
                        t_ps[:, c * 128:(c + 1) * 128],
                        xtm[:, sb * D + c * 128: sb * D + (c + 1) * 128],
                        ident[:, :])
                return inst
            tok_tr = pl.add("tensor", xtr_fn,
                            waits=[(dX1, tok_x), (vXE, last_vXE),
                                   (sQ, tok_wft), (gID, tok_ident)],
                            inc=tXT)
            xv = xfm[par].rearrange("p (c s n) -> p c s n", c=DC, s=4)
            last_vXE = pl.add(
                "vector", lambda e, sb=sb, xv=xv: e.tensor_copy(
                    xv[:, :, sb, :],
                    t_ps.ap().rearrange("p (c n) -> p c n", c=DC)),
                waits=[(tXT, tok_tr), (tPJ, prev_mm_group[par])], inc=vXE)
        prev_tr_group = tXT.count
        tok_xfm = last_vXE

        def proj_qk(wi, stage, bias_col, is_q, kind):
            toks = [0, 0]
            for dc2 in range(DC):
                half = dc2 % 2
                def mm_fn(e, wi=wi, dc2=dc2, half=half, par=par):
                    for di in range(DC):
                        inst = e.matmul(
                            p1_ps[half],
                            w3[wi][:, di * D + dc2 * 128:
                                   di * D + (dc2 + 1) * 128],
                            xfm[par][:, di * 512:(di + 1) * 512],
                            start=(di == 0), stop=(di == DC - 1))
                    return inst
                tok_mm = pl.add("tensor", mm_fn,
                                waits=[(vXE, tok_xfm),
                                       (sPJ, fps_half_tok_s[half]),
                                       (vPJ, fps_half_tok[half])], inc=tPJ)
                fn_e = lambda e, dc2=dc2, half=half, bias_col=bias_col, \
                    is_q=is_q: e.activation(
                        stage[:, dc2 * 512:(dc2 + 1) * 512],
                        p1_ps[half],
                        AF.Identity, bias=bias_col[:, dc2:dc2 + 1],
                        scale=QSCALE if is_q else 1.0)
                fps_half_tok_s[half] = pl.add(
                    "scalar", fn_e,
                    waits=[(tPJ, tok_mm), (dSCR, last_stage_dma[kind]),
                           (vQ, tok_bq32)],
                    inc=sPJ)
                toks[half] = fps_half_tok_s[half]
            return toks[(DC - 1) % 2], max(toks)

        _, tok_kev = proj_qk(1, kqstage, bk_sb, False, "kq")
        jt, hf = gr // 2, gr % 2

        def kdma_fn(e, jt=jt, hf=hf):
            for dc2 in range(DC):
                inst = e.dma_start(
                    out=k_scr[jt, dc2, :, hf * 512:(hf + 1) * 512],
                    in_=kqstage[:, dc2 * 512:(dc2 + 1) * 512])
                if dc2 < DC - 1:
                    inst.then_inc(dSCR.h, 16)
            return inst
        tok_kdma = pl.add("sync", kdma_fn,
                          waits=[(sPJ, tok_kev), (dSCR, dSCR.count)],
                          inc=dSCR, n=16, extra=16 * (DC - 1))
        last_stage_dma["kq"] = tok_kdma

        if own:
            _, tok_qev = proj_qk(0, kqstage, bq32_sb, True, "kq")

            def qdma_fn(e, gr=gr):
                for dc2 in range(DC):
                    inst = e.dma_start(
                        out=q_scr[dc2, :, gr * 512:(gr + 1) * 512],
                        in_=kqstage[:, dc2 * 512:(dc2 + 1) * 512])
                    if dc2 < DC - 1:
                        inst.then_inc(dSCR.h, 16)
                return inst
            tok_qdma = pl.add("sync", qdma_fn,
                              waits=[(sPJ, tok_qev), (dSCR, dSCR.count)],
                              inc=dSCR, n=16, extra=16 * (DC - 1))

        for sb in range(4):
            for nh in range(2):
                half = (sb * 2 + nh) % 2
                def vmm_fn(e, sb=sb, nh=nh, half=half, par=par):
                    for di in range(DC):
                        e.matmul(
                            p1_ps[half],
                            xfm[par][:, di * 512 + sb * 128:
                                     di * 512 + (sb + 1) * 128],
                            w3[2][:, di * D + nh * 512:
                                  di * D + (nh + 1) * 512],
                            start=(di == 0), stop=False)
                    inst = e.matmul(
                        p1_ps[half],
                        ones_r[:, :],
                        bv_row[:, nh * 512:(nh + 1) * 512],
                        start=False, stop=True)
                    return inst
                tok_mm = pl.add("tensor", vmm_fn,
                                waits=[(vXE, tok_xfm),
                                       (sPJ, fps_half_tok_s[half]),
                                       (vPJ, fps_half_tok[half])], inc=tPJ)
                fps_half_tok_s[half] = pl.add(
                    "scalar",
                    lambda e, sb=sb, nh=nh, half=half: e.copy(
                        vstage[:, sb * D + nh * 512: sb * D + (nh + 1) * 512],
                        p1_ps[half]),
                    waits=[(tPJ, tok_mm), (dSCR, last_stage_dma["v"])],
                    inc=sPJ)

        def vdma_fn(e, gr=gr):
            for sb in range(4):
                inst = e.dma_start(
                    out=v_scr[gr * 4 + sb, :, :],
                    in_=vstage[:, sb * D:(sb + 1) * D])
                if sb < 3:
                    inst.then_inc(dSCR.h, 16)
            return inst
        tok_vdma = pl.add("sync", vdma_fn,
                          waits=[(sPJ, max(fps_half_tok_s)),
                                 (dSCR, last_stage_dma["kq"])],
                          inc=dSCR, n=16, extra=16 * 3)
        last_stage_dma["v"] = tok_vdma
        last_stage_dma["kq"] = tok_vdma
        for sb in range(4):
            tok_scr_v[gr * 4 + sb] = tok_vdma
        if own:
            tok_scr_q = tok_vdma
        if hf == 1:
            tok_scr_k[jt] = tok_vdma
        prev_mm_group[par] = tPJ.count

    tok_p1_t = tPJ.count
    tok_p1_tr = tXT.count
    tok_p1_tr_evac = vXE.count
    tok_p1_d = dSCR.count

    if "qq" in taps:
        pl.add("sync", lambda e: e.dma_start(out=taps["qq"], in_=q_scr),
               waits=[(dSCR, tok_p1_d)], inc=dOUT, n=16)
    if "kk" in taps:
        pl.add("sync", lambda e: e.dma_start(out=taps["kk"], in_=k_scr),
               waits=[(dSCR, tok_p1_d)], inc=dOUT, n=16)
    if "vv" in taps:
        pl.add("sync", lambda e: e.dma_start(out=taps["vv"], in_=v_scr),
               waits=[(dSCR, tok_p1_d)], inc=dOUT, n=16)

    tok_tS = {}
    tok_vR = {}
    tok_sE = {}
    tok_sA = {}
    tok_tT = {}
    tok_sTE = {}
    tok_tAV = {}
    tok_vO = {}
    pair_of = {}
    pair_idx = 0

    last_k_user = [0, 0]
    last_v_user = [0, 0]
    tok_kload = [0, 0]
    tok_vload = [0, 0]
    tok_qload = 0
    tok_linv = {}

    def emit_k_load(g, j):
        kp = j % 2

        def kl_fn(e, j=j):
            for dc in range(DC):
                inst = e.dma_start(out=k_sb[kp][:, dc * KT:(dc + 1) * KT],
                                   in_=k_scr[j, dc, :, :])
                if dc < DC - 1:
                    inst.then_inc(dKV.h, 16)
            return inst
        tok_kload[kp] = pl.add(
            "sync", kl_fn,
            waits=[(tS, last_k_user[kp]), (dSCR, tok_scr_k[j]),
                   (tPJ, tok_p1_t), (tXT, tok_p1_tr),
                   (dKV, dKV.count)],
            inc=dKV, n=16, extra=16 * (DC - 1))

    def emit_v_load(g, j):
        vp = j % 2

        def vl_fn(e, j=j):
            for c in range(KT // 128):
                inst = e.dma_start(out=v_sb[vp][:, c * D:(c + 1) * D],
                                   in_=v_scr[j * (KT // 128) + c, :, :])
                if c < KT // 128 - 1:
                    inst.then_inc(dKVV.h, 16)
            return inst
        if j >= 2:
            vu = pair_of[(g, j - 2, NT_G - 1)] + 1
        elif g > 0:
            vu = pair_of[(g - 1, NKT - 2 + j, NT_G - 1)] + 1
        else:
            vu = 0
        tok_vload[vp] = pl.add(
            "sync", vl_fn,
            waits=[(tAV, vu), (dSCR, tok_scr_v[(j + 1) * 8 - 1]),
                   (tPJ, tok_p1_t), (tXT, tok_p1_tr),
                   (dKVV, dKVV.count)],
            inc=dKVV, n=16, extra=16 * (KT // 128 - 1))

    def emit_q_load(g):
        nonlocal tok_qload

        def ql_fn(e, g=g):
            for dc in range(DC):
                inst = e.dma_start(
                    out=q_fm[:, dc * QG:(dc + 1) * QG],
                    in_=q_scr[dc, :, g * QG:(g + 1) * QG])
                if dc < DC - 1:
                    inst.then_inc(dQL.h, 16)
            return inst
        prev_user = tok_tS.get((g - 1, NKT - 1, NT_G - 1), 0)
        tok_qload = pl.add("sync", ql_fn,
                           waits=[(tS, prev_user), (dSCR, tok_scr_q),
                                  (tPJ, tok_p1_t), (tXT, tok_p1_tr),
                                  (dQL, dQL.count)],
                           inc=dQL, n=16, extra=16 * (DC - 1))

    def emit_scores(cur, prev):
        g, j, i = cur
        jp = j % 2
        w = [(dKV, tok_kload[jp]), (dQL, tok_qload)]
        if prev is not None:
            w.append((sE, tok_sE[prev]))
            w.append((vR, tok_vR[prev]))

        def sc_fn(e, g=g, j=j, i=i, jp=jp):
            for nh in range(2):
                for dc in range(DC):
                    inst = e.matmul(
                        s_ps[:, nh * 512:(nh + 1) * 512],
                        q_fm[:, dc * QG + i * 128: dc * QG + (i + 1) * 128],
                        k_sb[jp][:, dc * KT + nh * 512:
                                 dc * KT + (nh + 1) * 512],
                        start=(dc == 0), stop=(dc == DC - 1))
            return inst
        tok_tS[cur] = pl.add("tensor", sc_fn, waits=w, inc=tS)
        last_k_user[jp] = tok_tS[cur]

    def emit_vm(cur):
        g, j, i = cur
        if j != 0:
            tok_vR[cur] = pl.add(
                "gpsimd", lambda e: e.nop(),
                waits=[(tS, tok_tS[cur])], inc=vR)
            return
        tok_vR[cur] = pl.add(
            "vector", lambda e, i=i: e.reduce_max(
                negm_g[:, i:i + 1], s_ps[:, :], AX.X, negate=True),
            waits=[(tS, tok_tS[cur])], inc=vR)

    def emit_exp(cur):
        g, j, i = cur
        t = pair_of[cur]
        tp = t % 2

        def ex_fn(e, tp=tp, i=i, lp=t % 3):
            return e.activation(at_sb[tp], s_ps[:, :], AF.Exp,
                                bias=negm_g[:, i:i + 1],
                                accum_out=lt_b[lp][:, :])
        tok_sE[cur] = pl.add(
            "scalar", ex_fn,
            waits=[(vR, tok_vR[cur]), (tT, max(0, t - 1)),
                   (vO, max(0, t - 2))], inc=sE)

    def emit_transp(prev):
        t = pair_of[prev]
        tp = t % 2

        def tr_fn(e, tp=tp):
            for c in range(DC):
                inst = e.transpose(t_ps[:, c * 128:(c + 1) * 128],
                                   at_sb[tp][:, c * 128:(c + 1) * 128],
                                   ident[:, :])
            return inst
        w_t = [(sE, tok_sE[prev]), (sTE, max(0, t)),
               (vXE, tok_p1_tr_evac)]
        if tps_guard[0]:
            w_t.append((sQ, tps_guard[0]))
        tok_tT[prev] = pl.add("tensor", tr_fn, waits=w_t, inc=tT)

    def emit_evacT(prev):
        t = pair_of[prev]
        tp = t % 2
        tok_sTE[prev] = pl.add(
            "scalar", lambda e, tp=tp: e.copy(att_sb[tp], t_ps[:, :]),
            waits=[(tT, tok_tT[prev]), (tAV, max(0, t - 1))], inc=sTE)

    def emit_av(prev):
        g, j, i = prev
        t = pair_of[prev]
        tp = t % 2
        vp = j % 2

        def av_fn(e, tp=tp, vp=vp, op_=t % 2):
            for nh in range(2):
                for kc in range(KT // 128):
                    inst = e.matmul(
                        o_ps[op_][:, nh * 512:(nh + 1) * 512],
                        att_sb[tp][:, kc * 128:(kc + 1) * 128],
                        v_sb[vp][:, kc * D + nh * 512:
                                 kc * D + (nh + 1) * 512],
                        start=(kc == 0), stop=(kc == KT // 128 - 1))
            return inst
        tok_tAV[prev] = pl.add(
            "tensor", av_fn,
            waits=[(sTE, tok_sTE[prev]), (vO, max(0, t - 1)),
                   (dKVV, tok_vload[vp])], inc=tAV)
        last_v_user[vp] = tok_tAV[prev]

    def emit_vo(prev):
        g, j, i = prev
        t = pair_of[prev]
        tp = t % 2

        def vo_fn(e, g=g, j=j, i=i, lp=t % 3, op_=t % 2):
            lnew = l_cur[(j + 1) % 2][:, i:i + 1]
            osl = o_acc[g % 2][:, i * D:(i + 1) * D]
            if j == 0:
                e.tensor_copy(lnew, lt_b[lp][:, :])
                e.drain()
                inst = e.tensor_copy(osl, o_ps[op_][:, :])
            else:
                lold = l_cur[j % 2][:, i:i + 1]
                e.tensor_add(lnew, lold, lt_b[lp][:, :])
                e.drain()
                inst = e.scalar_tensor_tensor(osl, osl, 1.0,
                                              o_ps[op_][:, :],
                                              OP.mult, OP.add)
            return inst
        w = [(tAV, tok_tAV[prev]), (sE, tok_sE[prev])]
        tok_vO[prev] = pl.add("vector", vo_fn, waits=w, inc=vO)

    def emit_cycle(cur, prev1, prev2):
        if prev1 is not None:
            emit_transp(prev1)
            emit_evacT(prev1)
        if cur is not None:
            emit_scores(cur, prev1)
        if prev2 is not None:
            emit_av(prev2)
        if cur is not None:
            emit_vm(cur)
        if prev2 is not None:
            emit_vo(prev2)
        if cur is not None:
            emit_exp(cur)

    last_xres_user = 0
    last_out_user = 0
    last_ftps_user = 0
    last_fps_user = 0
    last_pa_reader = [0, 0]
    h_tap_tok = [0, 0]
    tps_guard = [0]
    post_tok = {}

    def ln_chain(src, dst, box, final_wait=()):
        pl.add("vector", lambda e: e.reduce_sum(
            st_f[0][:, :], src, AX.X), inc=vQ)
        yield
        tok_mean = pl.add("vector", lambda e: e.tensor_scalar(
            st_f[1][:, :], st_f[0][:, :], 1.0 / D, None, OP.mult), inc=vQ)
        yield
        tok_sq = pl.add("scalar", lambda e: e.activation(
            sq_dump[:, :], src, AF.Square, accum_out=st_f[2][:, :]),
            waits=[(vQ, tok_mean)], inc=sQ)
        yield
        pl.add("vector", lambda e: e.tensor_mul(
            st_f[4][:, :], st_f[1][:, :], st_f[1][:, :]), inc=vQ)
        yield
        tok_v = pl.add("vector", lambda e: e.scalar_tensor_tensor(
            st_f[3][:, :], st_f[2][:, :], 1.0 / D, st_f[4][:, :],
            OP.mult, OP.subtract), waits=[(sQ, tok_sq)], inc=vQ)
        yield
        tok_ln = pl.add("scalar", lambda e: e.activation(
            st_f[4][:, :], st_f[3][:, :], AF.Ln, bias=cb_eps[:, :]),
            waits=[(vQ, tok_v)], inc=sQ)
        yield
        tok_r0 = pl.add("scalar", lambda e: e.activation(
            st_f[5][:, :], st_f[4][:, :], AF.Exp, scale=-0.5),
            waits=[(sQ, tok_ln)], inc=sQ)
        yield
        box["tok"] = pl.add("vector", lambda e: e.tensor_scalar(
            dst, src, st_f[1][:, :], st_f[5][:, :], OP.subtract, OP.mult),
            waits=list(final_wait) + [(sQ, tok_r0)], inc=vQ)

    def emit_post_a(g, i):
        nonlocal last_xres_user
        hp = i % 2
        row0 = g * QG + i * 128
        while (g, NKT - 1, i) not in tok_vO or (g, i) not in tok_linv:
            yield "stall"
        tok_x = pl.add("gpsimd", lambda e, row0=row0: e.dma_start(
            out=x_res[:, :], in_=x_in[row0:row0 + 128, :]),
            waits=[(vQ, last_xres_user)], inc=dXR, n=16)
        yield
        tok_r = pl.add(
            "vector", lambda e, g=g, i=i: e.scalar_tensor_tensor(
                t2_sb[:, :], o_acc[g % 2][:, i * D:(i + 1) * D],
                linv_g[:, i:i + 1], x_res[:, :], OP.mult, OP.add),
            waits=[(dXR, tok_x), (vQ, tok_linv[(g, i)]),
                   (vO, tok_vO[(g, NKT - 1, i)])], inc=vQ)
        last_xres_user = tok_r
        yield
        box = {}
        fw = [(dTAP, dTAP.count)] if "h1" in taps else ()
        yield from ln_chain(t2_sb[:, :], h_sb[hp][:, :], box, final_wait=fw)
        tok_h = box["tok"]
        if "h1" in taps:
            h_tap_tok[hp] = pl.add(
                "gpsimd", lambda e, row0=row0, hp=hp: e.dma_start(
                    out=taps["h1"][row0:row0 + 128, :], in_=h_sb[hp][:, :]),
                waits=[(vQ, tok_h)], inc=dTAP, n=16)
            yield
        pl.add("vector", lambda e, hp=hp: e.tensor_reduce(
            am_t.ap().unsqueeze(2),
            h_sb[hp].ap().rearrange("p (b k) -> p b k", k=32),
            AX.X, OP.max, apply_absolute_value=True), inc=vQ)
        yield
        box2 = {}
        yield from emit_block_exponent(am_t, bias_act, None, None, [], box2)
        tok_sc = box2["tok_s"]
        box3 = {}
        yield from emit_quant_core(
            h_sb[hp][:, :], pa_sb[hp],
            [(sQ, tok_sc), (tF, last_pa_reader[hp])], box3)
        post_tok[("q", g, i)] = box3["tok"]

    def emit_post_b(g, i):
        nonlocal last_out_user, last_ftps_user, last_fps_user
        hp = i % 2
        row0 = g * QG + i * 128

        def ftr_fn(e, hp=hp):
            for c in range(DC):
                inst = e.transpose(t_ps[:, c * 128:(c + 1) * 128],
                                   pa_sb[hp][:, c * 128:(c + 1) * 128],
                                   ident[:, :])
            return inst
        tok_tr = pl.add("tensor", ftr_fn,
                        waits=[(vQ, post_tok[("q", g, i)]),
                               (sQ, last_ftps_user),
                               (sTE, sTE.count)], inc=tF)
        post_tok[("tr", g, i)] = tok_tr
        last_pa_reader[hp] = tok_tr
        tok_ev = pl.add("scalar", lambda e: e.copy(
            xqt_sb, t_ps[:, :]), waits=[(tF, tok_tr)], inc=sQ)
        last_ftps_user = tok_ev
        tps_guard[0] = tok_ev
        yield
        for nh in range(2):
            def fmm_fn(e, nh=nh):
                for dc in range(DC):
                    inst = e.matmul(
                        f_ps[:, :],
                        xqt_sb[:, dc * 128:(dc + 1) * 128],
                        wf_t[:, dc * D + nh * 512: dc * D + (nh + 1) * 512],
                        start=(dc == 0), stop=(dc == DC - 1))
                return inst
            tok_fm = pl.add("tensor", fmm_fn,
                            waits=[(sQ, tok_ev), (vQ, last_fps_user)],
                            inc=tF)
            yield
            last_fps_user = pl.add(
                "vector", lambda e, nh=nh: e.scalar_tensor_tensor(
                    t2_sb[:, nh * 512:(nh + 1) * 512], f_ps[:, :],
                    c_a[:, :], bf_sb[:, nh * 512:(nh + 1) * 512],
                    OP.mult, OP.add),
                waits=[(tF, tok_fm), (sQ, tok_ca)], inc=vQ)
            yield
        pl.add("vector", lambda e, hp=hp: e.tensor_tensor(
            qs_f[0][:, :], t2_sb[:, :], h_sb[hp][:, :], OP.add), inc=vQ)
        yield
        box = {}
        yield from ln_chain(qs_f[0][:, :], out_sb[:, :], box,
                            final_wait=[(dOUT, last_out_user)])
        tok_out = box["tok"]
        last_out_user = pl.add(
            "gpsimd", lambda e, row0=row0: e.dma_start(
                out=out[row0:row0 + 128, :], in_=out_sb[:, :]),
            waits=[(vQ, tok_out)], inc=dOUT, n=16)

    def post_group_gen(g):
        yield from emit_post_a(g, 0)
        yield from emit_post_a(g, 1)
        yield from emit_post_b(g, 0)
        yield from emit_post_a(g, 2)
        yield from emit_post_b(g, 1)
        yield from emit_post_a(g, 3)
        yield from emit_post_b(g, 2)
        yield from emit_post_b(g, 3)

    for g in range(NQG):
        for j in range(NKT):
            for i in range(NT_G):
                pair_of[(g, j, i)] = pair_idx
                pair_idx += 1

    PACE = 8
    post_gen = [None]

    def dispense(n):
        if post_gen[0] is None:
            return
        for _ in range(n):
            r = next(post_gen[0], "END")
            if r == "END":
                post_gen[0] = None
                return
            if r == "stall":
                return

    def after_vo(p2):
        if p2 is None or p2[1] != NKT - 1:
            return
        fg, i = p2[0], p2[2]
        tok_linv[(fg, i)] = pl.add(
            "vector", lambda e, i=i: _recip(e, linv_g[:, i:i + 1],
                                            l_cur[NKT % 2][:, i:i + 1]),
            waits=[(vO, tok_vO[p2])], inc=vQ)
        if i == 0:
            while post_gen[0] is not None:
                dispense(10 ** 9)
            post_gen[0] = post_group_gen(fg)

    prev1 = None
    prev2 = None
    for g in range(NQG):
        emit_q_load(g)
        if g == 0:
            emit_k_load(0, 0)
            emit_v_load(0, 0)
        for j in range(NKT):
            for i in range(NT_G):
                cur = (g, j, i)
                emit_cycle(cur, prev1, prev2)
                after_vo(prev2)
                prev2 = prev1
                prev1 = cur
                dispense(PACE)
            if j + 1 < NKT:
                emit_k_load(g, j + 1)
                emit_v_load(g, j + 1)
            elif g + 1 < NQG:
                emit_k_load(g + 1, 0)
                emit_v_load(g + 1, 0)

    emit_cycle(None, prev1, prev2)
    after_vo(prev2)
    emit_cycle(None, None, prev1)
    after_vo(prev1)
    while post_gen[0] is not None:
        dispense(10 ** 9)

    pl.add("sync", lambda e: e.nop(), waits=[(dOUT, dOUT.count)], inc=None)

    with nc.Block() as block:
        pl.emit(block)
    return nc


_NC_CACHE = {}


def _get_program(debug_taps=()):
    key = tuple(sorted(debug_taps))
    if key not in _NC_CACHE:
        _NC_CACHE[key] = build_program(debug_taps=key)
    return _NC_CACHE[key]


def kernel(L0_attn_X, L0_attn_W_q, L0_attn_W_k, L0_attn_W_v, L0_attn_b_q,
           L0_attn_b_k, L0_attn_b_v, L0_ffn_weight, L0_ffn_bias,
           L0_ffn_delta_w, L0_ffn_delta_act, _debug_taps=(), _cores=None):
    X = np.asarray(L0_attn_X, dtype=np.float32)
    nc = _get_program(_debug_taps)
    cores = list(range(NCORES)) if _cores is None else _cores
    in_maps = []
    for c in cores:
        b, h = c // 2, c % 2
        xb = X[b]
        if h == 1:
            xb = np.roll(xb, -OWN, axis=0)
        in_maps.append({
            "x_in": np.ascontiguousarray(xb),
            "w_q": np.ascontiguousarray(L0_attn_W_q, dtype=np.float32),
            "w_k": np.ascontiguousarray(L0_attn_W_k, dtype=np.float32),
            "w_v": np.ascontiguousarray(L0_attn_W_v, dtype=np.float32),
            "b_q": np.ascontiguousarray(L0_attn_b_q, dtype=np.float32),
            "b_k": np.ascontiguousarray(L0_attn_b_k, dtype=np.float32),
            "b_v": np.ascontiguousarray(L0_attn_b_v, dtype=np.float32),
            "w_f": np.ascontiguousarray(L0_ffn_weight, dtype=np.float32),
            "b_f": np.ascontiguousarray(L0_ffn_bias, dtype=np.float32),
            "d_w": np.ascontiguousarray(L0_ffn_delta_w, dtype=np.float32),
            "d_a": np.ascontiguousarray(L0_ffn_delta_act, dtype=np.float32),
        })
    res = run_bass_kernel_spmd(nc, in_maps, core_ids=list(range(len(cores))))
    if _debug_taps or _cores is not None:
        return res
    outp = np.empty((4, S, D), dtype=np.float32)
    for idx, c in enumerate(cores):
        b, h = c // 2, c % 2
        outp[b, h * OWN:(h + 1) * OWN] = np.asarray(res.results[idx]["out"])
    return outp
